# version 06 (final)
# baseline (speedup 1.0000x reference)
import os
os.environ.setdefault("NEURON_SCRATCHPAD_PAGE_SIZE", "768")
import math
import numpy as np
from contextlib import ExitStack

import concourse.bass as bass
import concourse.mybir as mybir
from concourse.bass_utils import run_bass_kernel_spmd

F32 = mybir.dt.float32
BF16 = mybir.dt.bfloat16
AF = mybir.ActivationFunctionType
ALU = mybir.AluOpType
AX = mybir.AxisListType

T = 2048
D = 2048
NCH = 16
TQ = 512
NTQ = 4
H0 = 8192
NH0 = 64
H1 = 5632
NH1 = 44
KV = 512
LQG = 4096

RS2048 = 1.0 / math.sqrt(2048.0)
RS8192 = 1.0 / math.sqrt(8192.0)
RS128 = 1.0 / math.sqrt(128.0)

SUBLAYERS = ("ffn0", "att", "ffn0", "ffn1", "scan", "ffn1")
UNFUSED = False



def host_consts():
    H = np.array([[1.0]], dtype=np.float32)
    for _ in range(7):
        H = np.block([[H, H], [H, -H]]).astype(np.float32)
    i = np.arange(1, 65, dtype=np.float32)
    thetas = np.float32(10000.0) ** (-2.0 * (i - 1.0) / np.float32(128.0))
    ang = np.arange(1, T + 1, dtype=np.float32)[:, None] * thetas
    cosx = np.concatenate([np.cos(ang), np.cos(ang)], axis=1).astype(np.float32).T
    sinx = np.concatenate([np.sin(ang), np.sin(ang)], axis=1).astype(np.float32).T
    maskc = np.where(np.arange(128)[:, None] >= np.arange(128)[None, :],
                     0.0, -1e30).astype(np.float32)
    return {
        "ident": np.ascontiguousarray(np.eye(128, dtype=np.float32)),
        "h128": np.ascontiguousarray(H),
        "cosx": np.ascontiguousarray(cosx),
        "sinx": np.ascontiguousarray(sinx),
        "maskc": np.ascontiguousarray(maskc),
        "ones_a": np.ones((128, 1), dtype=np.float32),
        "ones_b": np.ones((1, 128), dtype=np.float32),
    }


def vec_chunks(v):
    v = np.asarray(v, dtype=np.float32)
    n = v.shape[0] // 128
    return np.ascontiguousarray(v.reshape(n, 128).T)




class K:
    def __init__(self, nc, es):
        self.nc = nc
        self.sem = {}
        self.c = {}
        for s in ("pe", "act", "dve", "pool", "sd", "gd", "ad"):
            self.sem[s] = es.enter_context(nc.semaphore("sem_" + s))
            self.c[s] = 0

    def inc(self, ins, s, n=1):
        ins.then_inc(self.sem[s], n)
        self.c[s] += n
        return self.c[s]

    def w(self, eng, s, v):
        if v > 0:
            eng.wait_ge(self.sem[s], v)

    def fence(self):
        with self.nc.Block() as blk:
            gd, sd, ad = self.c["gd"], self.c["sd"], self.c["ad"]

            @blk.gpsimd
            def _(e):
                self.w(e, "gd", gd)

            @blk.sync
            def _(e):
                self.w(e, "sd", sd)

            @blk.scalar
            def _(e):
                self.w(e, "ad", ad)


_SBUF_SERIAL = [0]


def sbuf(K_, es, t, name, shape, dt):
    _SBUF_SERIAL[0] += 1
    t[name] = es.enter_context(
        K_.nc.sbuf_tensor(f"{name}_{_SBUF_SERIAL[0]}", shape, dt))




def p0_emit(K_, t, es_p0, need):
    nc = K_.nc
    with nc.Block() as blk:
        @blk.gpsimd
        def _(g):
            casts = []
            if "ffn0" in need:
                casts += [("wbf_up", "wup"), ("wbf_gate", "wgate")]
            if "att" in need:
                casts += [("wbf_q", "wq"), ("wbf_k", "wk"), ("wbf_v", "wv"),
                          ("wbf_o", "wo")]
            if "scan" in need:
                casts += [("wbf_l1", "wl1")]
            for a, b in casts:
                K_.inc(g.dma_start(t[a][:, :], t[b]), "gd", 16)
            if "att" in need:
                K_.inc(g.dma_start(t["sb_cos"][:], t["cosx"]), "gd", 16)
                K_.inc(g.dma_start(t["sb_sin"][:], t["sinx"]), "gd", 16)
            K_.inc(g.dma_start(t["sb_identbf"][:], t["ident"]), "gd", 16)

        @blk.sync
        def _(s):
            for name in ("bqv", "bkv", "bvv", "bov", "suug", "sudp", "wscu",
                         "wscg", "wscdp", "h128", "maskc", "ones_a", "ones_b"):
                K_.inc(s.dma_start(t["sb_" + name][:], t[name]), "sd", 16)

    K_.fence()

    with nc.Block() as blk:
        @blk.vector
        def _(v):
            for i_ in range(6):
                v.memset(t["ps%d" % i_][:], 0.0)
            v.memset(t["sb_eps"][:], 1e-5)
            v.tensor_scalar_mul(t["sb_wscu_su"][:], t["sb_wscu"][:], RS2048)
            v.tensor_mul(t["sb_wscu_su"][:], t["sb_wscu_su"][:], t["sb_sudp"][:])
            v.tensor_scalar_mul(t["sb_wscg_s"][:], t["sb_wscg"][:], RS2048)
            v.tensor_scalar_mul(t["sb_wscdp_h"][:], t["sb_wscdp"][:], 0.5)
            v.tensor_scalar_mul(t["sb_bq_s"][:], t["sb_bqv"][:], RS128)
            for c in range(NCH):
                v.tensor_scalar(t["sb_h128su"][:, c * 128:(c + 1) * 128],
                                t["sb_h128"][:], t["sb_suug"][:, c:c + 1], None,
                                ALU.mult)
            v.tensor_copy(t["sb_h128bf"][:], t["sb_h128"][:])

    if "ffn0" in need:
        with ExitStack() as es:
            sbuf(K_, es, t, "p0_bufA0", [128, H0], F32)
            sbuf(K_, es, t, "p0_bufA1", [128, H0], F32)
            sbuf(K_, es, t, "p0_bufB", [128, H0], F32)
            sbuf(K_, es, t, "p0_bufC0", [128, H0], BF16)
            sbuf(K_, es, t, "p0_bufC1", [128, H0], BF16)
            ldw = {}
            stw = {}
            for ch in range(NCH):
                with nc.Block() as blk:
                    bufA = t["p0_bufA%d" % (ch % 2)]
                    bufC = t["p0_bufC%d" % (ch % 2)]

                    @blk.gpsimd
                    def _(g):
                        if ch == 0:
                            ldw[0] = K_.inc(g.dma_start(
                                t["p0_bufA0"][:], t["wdown"][0:128, :]),
                                "gd", 16)
                        if ch + 1 < NCH:
                            ldw[ch + 1] = K_.inc(g.dma_start(
                                t["p0_bufA%d" % ((ch + 1) % 2)][:],
                                t["wdown"][(ch + 1) * 128:(ch + 2) * 128, :]),
                                "gd", 16)

                    @blk.vector
                    def _(v):
                        v.wait_ge(K_.sem["gd"], ldw[ch])
                        src, dst = bufA, t["p0_bufB"]
                        for s_ in range(6):
                            h = 128 << s_
                            av = src[:].rearrange("p (nb two h) -> p nb two h",
                                                  two=2, h=h)
                            bv = dst[:].rearrange("p (nb two h) -> p nb two h",
                                                  two=2, h=h)
                            v.tensor_add(bv[:, :, 0, :], av[:, :, 0, :],
                                         av[:, :, 1, :])
                            v.tensor_sub(bv[:, :, 1, :], av[:, :, 0, :],
                                         av[:, :, 1, :])
                            src, dst = dst, src
                        if ch >= 2:
                            v.wait_ge(K_.sem["sd"], stw[ch - 2])
                        K_.inc(v.tensor_copy(bufC[:], src[:]), "dve", 1)

                    @blk.sync
                    def _(sy):
                        sy.wait_ge(K_.sem["dve"], K_.c["dve"])
                        stw[ch] = K_.inc(sy.dma_start(
                            t["wbf_down"][ch * 128:(ch + 1) * 128, :],
                            bufC[:]), "sd", 16)
            K_.fence()

    if "ffn1" in need:
        with ExitStack() as es:
            sbuf(K_, es, t, "p0_qb", [128, 2 * H1], F32)
            sbuf(K_, es, t, "p0_scrap", [128, 2 * H1], F32)
            sbuf(K_, es, t, "p0_parts", [128, 64], F32)
            sbuf(K_, es, t, "p0_g1", [128, H1], F32)
            sbuf(K_, es, t, "p0_obuf", [128, 2 * H1], BF16)
            sbuf(K_, es, t, "p0_scbc", [128, 96], F32)
            QSPEC = (("w1", "wbf_w1", NH1, 2048), ("w2", "wbf_w2", NH1, 2048),
                     ("w3", "wbf_w3", NCH, H1))
            for qi, (wname, dname, nch, ncols) in enumerate(QSPEC):
                _p0_quant_gamma(K_, t, wname, qi, nch, ncols)
            K_.fence()
            for qi, (wname, dname, nch, ncols) in enumerate(QSPEC):
                _p0_quant_apply(K_, t, wname, dname, qi, nch, ncols)
            K_.fence()

    K_.fence()


def _p0_quant_gamma(K_, t, wname, qi, nch, ncols):
    nc = K_.nc
    qb, scrap, parts = t["p0_qb"], t["p0_scrap"], t["p0_parts"]
    nelem = float(nch * 128 * ncols)

    with nc.Block() as blk:
        ld_ev = []
        b_act = K_.c["act"]
        b_dve_q = K_.c["dve"]

        @blk.gpsimd
        def _(g):
            for ch in range(nch):
                if ch >= 2:
                    g.wait_ge(K_.sem["act"], b_act + ch - 1)
                ld_ev.append(K_.inc(
                    g.dma_start(qb[:, (ch % 2) * ncols:(ch % 2 + 1) * ncols],
                                t[wname][ch * 128:(ch + 1) * 128, :]), "gd", 16))

        @blk.scalar
        def _(s):
            for ch in range(nch):
                s.wait_ge(K_.sem["gd"], ld_ev[ch])
                if ch >= 2:
                    s.wait_ge(K_.sem["dve"], b_dve_q + ch - 1)
                K_.inc(s.activation(scrap[:, (ch % 2) * ncols:(ch % 2) * ncols + ncols],
                                    qb[:, (ch % 2) * ncols:(ch % 2 + 1) * ncols],
                                    AF.Abs), "act", 1)

        @blk.vector
        def _(v):
            for ch in range(nch):
                v.wait_ge(K_.sem["act"], b_act + ch + 1)
                K_.inc(v.reduce_sum(t["ps3"][:, ch:ch + 1],
                                    scrap[:, (ch % 2) * ncols:(ch % 2) * ncols + ncols],
                                    AX.X), "dve", 1)

    with nc.Block() as blk:
        @blk.vector
        def _(v):
            K_.inc(v.reduce_sum(t["ps2"][:, 8:9], t["ps3"][:, 0:nch], AX.X),
                   "dve", 1)

        @blk.scalar
        def _(s):
            s.wait_ge(K_.sem["dve"], K_.c["dve"])
            K_.inc(s.copy(scrap[:, 0:512], t["ps2"][:, 0:512]), "act", 1)

    with nc.Block() as blk:
        @blk.tensor
        def _(tn):
            tn.wait_ge(K_.sem["dve"], K_.c["dve"])
            K_.inc(tn.matmul(t["ps0"][0:1, 0:512], t["sb_ones_a"][:],
                             scrap[:, 0:512], start=True, stop=True), "pe", 1)

        @blk.vector
        def _(v):
            v.wait_ge(K_.sem["pe"], K_.c["pe"])
            v.tensor_scalar_mul(scrap[0:1, 1024:1536], t["ps0"][0:1, 0:512],
                                1.0 / nelem)
            K_.inc(v.tensor_scalar_add(scrap[0:1, 1024:1536],
                                       scrap[0:1, 1024:1536], 1e-8), "dve", 1)

        @blk.scalar
        def _(s):
            s.wait_ge(K_.sem["dve"], K_.c["dve"])
            s.activation(scrap[0:1, 1536:2048], scrap[0:1, 1024:1536], AF.Abs)
            s.activation(scrap[0:1, 1024:1536], scrap[0:1, 1536:2048], AF.Ln)
            K_.inc(s.activation(scrap[0:1, 1536:2048], scrap[0:1, 1024:1536],
                                AF.Exp, scale=-1.0), "act", 1)

        @blk.vector
        def _(v):
            v.wait_ge(K_.sem["act"], K_.c["act"])
            K_.inc(v.tensor_copy(scrap[0:1, 2560:3072],
                                 scrap[0:1, 1536:2048]), "dve", 1)

        @blk.tensor
        def _(tn):
            tn.wait_ge(K_.sem["dve"], K_.c["dve"])
            K_.inc(tn.matmul(t["ps1"][:, 0:512], t["sb_ones_b"][:],
                             scrap[0:1, 2560:3072], start=True, stop=True),
                   "pe", 1)

        @blk.scalar
        def _(s):
            s.wait_ge(K_.sem["pe"], K_.c["pe"])
            K_.inc(s.copy(t["p0_scbc"][:, qi * 32:(qi + 1) * 32],
                          t["ps1"][:, 8:40]), "act", 1)

        if "qdbg" in t:
            @blk.vector
            def _(v):
                v.wait_ge(K_.sem["act"], K_.c["act"])
                v.tensor_copy(scrap[0:1, 4096 + 0:4096 + 16], t["ps0"][0:1, 0:16])
                v.tensor_copy(scrap[0:1, 4096 + 16:4096 + 32],
                              scrap[0:1, 1024 + 0:1024 + 16])
                v.tensor_copy(scrap[0:1, 4096 + 32:4096 + 48],
                              scrap[0:1, 2560 + 0:2560 + 16])
                K_.inc(v.tensor_copy(scrap[0:1, 4096 + 48:4096 + 64],
                                     t["ps1"][0:1, 0:16]), "dve", 1)

            @blk.gpsimd
            def _(g):
                g.wait_ge(K_.sem["dve"], K_.c["dve"])
                K_.inc(g.dma_start(t["qdbg"][:, qi * 256:qi * 256 + 96],
                                   t["p0_scbc"][:, 0:96]), "gd", 16)
                K_.inc(g.dma_start(t["qdbg"][0:2, qi * 256 + 96:qi * 256 + 96 + 128],
                                   scrap[0:2, 4096:4096 + 128]), "gd", 16)


def _p0_quant_apply(K_, t, wname, dname, qi, nch, ncols):
    nc = K_.nc
    qb, scrap = t["p0_qb"], t["p0_scrap"]
    with nc.Block() as blk:
        gd0 = K_.c["gd"]
        b_dve = K_.c["dve"]

        @blk.gpsimd
        def _(g):
            for ch in range(nch):
                if ch >= 2:
                    g.wait_ge(K_.sem["dve"], b_dve + ch - 1)
                K_.inc(g.dma_start(qb[:, (ch % 2) * ncols:(ch % 2 + 1) * ncols],
                                   t[wname][ch * 128:(ch + 1) * 128, :]),
                       "gd", 16)

        sd0 = K_.c["sd"]

        @blk.vector
        def _(v):
            for ch in range(nch):
                sl = qb[:, (ch % 2) * ncols:(ch % 2 + 1) * ncols]
                osl = t["p0_obuf"][:, (ch % 2) * ncols:(ch % 2 + 1) * ncols]
                v.wait_ge(K_.sem["gd"], gd0 + 16 * (ch + 1))
                if ch >= 2:
                    v.wait_ge(K_.sem["sd"], sd0 + 16 * (ch - 1))
                v.tensor_scalar(sl, sl, t["p0_scbc"][:, qi * 32:qi * 32 + 1],
                                None, ALU.mult)
                v.tensor_single_scalar(scrap[:, 0:ncols], sl, 0.5, ALU.is_gt)
                v.tensor_single_scalar(t["p0_g1"][:, 0:ncols], sl, -0.5, ALU.is_lt)
                K_.inc(v.tensor_sub(osl, scrap[:, 0:ncols], t["p0_g1"][:, 0:ncols]),
                       "dve", 1)

        @blk.sync
        def _(sy):
            for ch in range(nch):
                sy.wait_ge(K_.sem["dve"], b_dve + ch + 1)
                K_.inc(sy.dma_start(
                    t[dname][ch * 128:(ch + 1) * 128, :],
                    t["p0_obuf"][:, (ch % 2) * ncols:(ch % 2 + 1) * ncols]),
                    "sd", 16)




def emit_ln_tq(K_, blk, t, x_dram, tq, dst, cs, off):
    ld_ev = []
    b_pe = K_.c["pe"]
    b_act = K_.c["act"]
    xq, sq, st = t["ln_xq"], t["ln_sq"], t["ln_st"]

    @blk.gpsimd
    def _(g):
        for ch in range(NCH):
            ld_ev.append(K_.inc(
                g.dma_start(xq[:, ch * TQ:(ch + 1) * TQ],
                            x_dram[ch * 128:(ch + 1) * 128,
                                   tq * TQ:(tq + 1) * TQ]), "gd", 16))

    @blk.scalar
    def _(s):
        for ch in range(NCH):
            s.wait_ge(K_.sem["gd"], ld_ev[ch])
            if ch >= 2:
                s.wait_ge(K_.sem["pe"], b_pe + ch - 1)
            K_.inc(s.activation(sq[:, (ch % 2) * TQ:(ch % 2 + 1) * TQ],
                                xq[:, ch * TQ:(ch + 1) * TQ], AF.Square),
                   "act", 1)

    @blk.tensor
    def _(tn):
        for ch in range(NCH):
            tn.wait_ge(K_.sem["gd"], ld_ev[ch])
            tn.wait_ge(K_.sem["act"], b_act + ch + 1)
            tn.matmul(t["ps0"][0:1, :], t["sb_ones_a"][:],
                      xq[:, ch * TQ:(ch + 1) * TQ],
                      start=(ch == 0), stop=(ch == NCH - 1))
            K_.inc(tn.matmul(t["ps1"][0:1, :], t["sb_ones_a"][:],
                             sq[:, (ch % 2) * TQ:(ch % 2 + 1) * TQ],
                             start=(ch == 0), stop=(ch == NCH - 1)), "pe", 1)

    @blk.vector
    def _(v):
        v.wait_ge(K_.sem["pe"], K_.c["pe"])
        m, msq, var = st[0:1, 0:TQ], st[0:1, TQ:2 * TQ], st[0:1, 2 * TQ:3 * TQ]
        v.tensor_scalar_mul(m, t["ps0"][0:1, :], 1.0 / D)
        v.tensor_scalar_mul(msq, t["ps1"][0:1, :], 1.0 / D)
        v.tensor_mul(var, m, m)
        K_.inc(v.tensor_sub(var, msq, var), "dve", 1)

    @blk.scalar
    def _(s):
        s.wait_ge(K_.sem["dve"], K_.c["dve"])
        s.activation(st[0:1, 3 * TQ:4 * TQ], st[0:1, 2 * TQ:3 * TQ],
                     AF.Ln, bias=t["sb_eps"][0:1, 0:1])
        K_.inc(s.activation(st[0:1, 4 * TQ:5 * TQ], st[0:1, 3 * TQ:4 * TQ],
                            AF.Exp, scale=-0.5), "act", 1)

    @blk.vector
    def _(v):
        v.wait_ge(K_.sem["act"], K_.c["act"])
        rstd, nmr = st[0:1, 4 * TQ:5 * TQ], st[0:1, 5 * TQ:6 * TQ]
        v.tensor_mul(nmr, st[0:1, 0:TQ], rstd)
        K_.inc(v.tensor_scalar_mul(nmr, nmr, -1.0), "dve", 1)

    @blk.tensor
    def _(tn):
        tn.wait_ge(K_.sem["dve"], K_.c["dve"])
        tn.matmul(t["ps2"][:], t["sb_ones_b"][:], st[0:1, 4 * TQ:5 * TQ],
                  start=True, stop=True)
        K_.inc(tn.matmul(t["ps3"][:], t["sb_ones_b"][:], st[0:1, 5 * TQ:6 * TQ],
                         start=True, stop=True), "pe", 1)

    ln_ev = []

    @blk.vector
    def _(v):
        v.wait_ge(K_.sem["pe"], K_.c["pe"])
        for ch in range(NCH):
            v.tensor_mul(t["ln_tmp"][:], xq[:, ch * TQ:(ch + 1) * TQ], t["ps2"][:])
            ln_ev.append(K_.inc(
                v.tensor_add(dst[:, ch * cs + off:ch * cs + off + TQ],
                             t["ln_tmp"][:], t["ps3"][:]), "dve", 1))

    return ln_ev


def alloc_ln_ws(K_, es, t):
    sbuf(K_, es, t, "ln_xq", [128, 16 * TQ], F32)
    sbuf(K_, es, t, "ln_sq", [128, 2 * TQ], F32)
    sbuf(K_, es, t, "ln_st", [1, 8 * TQ], F32)
    sbuf(K_, es, t, "ln_tmp", [128, TQ], F32)




def emit_ffn0(K_, t, x_in, x_out):
    nc = K_.nc
    for tq in range(NTQ):
        with ExitStack() as es_tq:
            sbuf(K_, es_tq, t, "f0_a", [128, 16 * TQ], BF16)
            sbuf(K_, es_tq, t, "f0_c", [128, NH0 * TQ], BF16)
            K_.fence()
            with ExitStack() as es_a:
                alloc_ln_ws(K_, es_a, t)
                sbuf(K_, es_a, t, "f0_bt", [128, TQ], BF16)
                with nc.Block() as blk:
                    ln_ev = emit_ln_tq(K_, blk, t, x_in, tq, t["f0_a"], TQ, 0)
                    b_pe = K_.c["pe"]
                    b_act = K_.c["act"]

                    @blk.tensor
                    def _(tn):
                        for ch in range(NCH):
                            tn.wait_ge(K_.sem["dve"], ln_ev[ch])
                            if ch >= 2:
                                tn.wait_ge(K_.sem["act"], b_act + ch - 1)
                            K_.inc(tn.matmul(
                                t["ps4" if ch % 2 == 0 else "ps5"][:],
                                t["sb_h128su"][:, ch * 128:(ch + 1) * 128],
                                t["f0_a"][:, ch * TQ:(ch + 1) * TQ],
                                start=True, stop=True), "pe", 1)

                    @blk.scalar
                    def _(s):
                        for ch in range(NCH):
                            s.wait_ge(K_.sem["pe"], b_pe + ch + 1)
                            K_.inc(s.copy(t["f0_a"][:, ch * TQ:(ch + 1) * TQ],
                                          t["ps4" if ch % 2 == 0 else "ps5"][:]),
                                   "act", 1)

                    @blk.vector
                    def _(v):
                        v.wait_ge(K_.sem["act"], K_.c["act"])
                        a, bt = t["f0_a"], t["f0_bt"]
                        for s_ in range(4):
                            hc = 1 << s_
                            for b in range(NCH // (2 * hc)):
                                for j in range(hc):
                                    lo = (b * 2 * hc + j) * TQ
                                    hi = lo + hc * TQ
                                    v.tensor_sub(bt[:], a[:, lo:lo + TQ],
                                                 a[:, hi:hi + TQ])
                                    v.tensor_add(a[:, lo:lo + TQ],
                                                 a[:, lo:lo + TQ],
                                                 a[:, hi:hi + TQ])
                                    v.tensor_copy(a[:, hi:hi + TQ], bt[:])

            K_.fence()
            with ExitStack() as es_b:
                sbuf(K_, es_b, t, "f0_wu0", [128, 16 * TQ], BF16)
                sbuf(K_, es_b, t, "f0_wu1", [128, 16 * TQ], BF16)
                sbuf(K_, es_b, t, "f0_wg0", [128, 16 * TQ], BF16)
                sbuf(K_, es_b, t, "f0_wg1", [128, 16 * TQ], BF16)
                sbuf(K_, es_b, t, "f0_sil0", [128, TQ], F32)
                sbuf(K_, es_b, t, "f0_sil1", [128, TQ], F32)
                sbuf(K_, es_b, t, "f0_cp0", [128, TQ], BF16)
                sbuf(K_, es_b, t, "f0_cp1", [128, TQ], BF16)
                ldu = {}
                ldg = {}
                NMS = NH0 // 4
                with nc.Block() as blk:
                    b_pe = K_.c["pe"]
                    b_act = K_.c["act"]
                    b_dve = K_.c["dve"]

                    @blk.sync
                    def _(sy):
                        for ms in range(NMS):
                            if ms >= 2:
                                sy.wait_ge(K_.sem["pe"],
                                           b_pe + 2 * (4 * (ms - 1)) + 1)
                            wu = t["f0_wu0"] if ms % 2 == 0 else t["f0_wu1"]
                            wg = t["f0_wg0"] if ms % 2 == 0 else t["f0_wg1"]
                            for kc in range(NCH):
                                K_.inc(sy.dma_start_transpose(
                                    wu[:, kc * TQ:(kc + 1) * TQ],
                                    t["wbf_up"][ms * 512:(ms + 1) * 512,
                                                kc * 128:(kc + 1) * 128]),
                                    "sd", 16)
                                ldu[ms] = K_.inc(sy.dma_start_transpose(
                                    wg[:, kc * TQ:(kc + 1) * TQ],
                                    t["wbf_gate"][ms * 512:(ms + 1) * 512,
                                                  kc * 128:(kc + 1) * 128]),
                                    "sd", 16)

                    @blk.tensor
                    def _(tn):
                        for mc in range(NH0):
                            ms, mj = mc // 4, mc % 4
                            wu = t["f0_wu0"] if ms % 2 == 0 else t["f0_wu1"]
                            wg = t["f0_wg0"] if ms % 2 == 0 else t["f0_wg1"]
                            psU = t["ps0"] if mc % 2 == 0 else t["ps1"]
                            psG = t["ps2"] if mc % 2 == 0 else t["ps3"]
                            psH = t["ps4"] if mc % 2 == 0 else t["ps5"]
                            tn.wait_ge(K_.sem["sd"], ldu[ms])
                            if mc >= 2:
                                tn.wait_ge(K_.sem["dve"], b_dve + mc - 1)
                                tn.wait_ge(K_.sem["act"], b_act + 2 * mc - 3)
                            for kc in range(NCH):
                                tn.matmul(psU[:],
                                          wu[:, kc * TQ + mj * 128:
                                             kc * TQ + (mj + 1) * 128],
                                          t["f0_a"][:, kc * TQ:(kc + 1) * TQ],
                                          start=(kc == 0), stop=(kc == NCH - 1))
                            for kc in range(NCH):
                                ins = tn.matmul(psG[:],
                                                wg[:, kc * TQ + mj * 128:
                                                   kc * TQ + (mj + 1) * 128],
                                                t["f0_a"][:, kc * TQ:(kc + 1) * TQ],
                                                start=(kc == 0),
                                                stop=(kc == NCH - 1))
                            K_.inc(ins, "pe", 1)
                            tn.wait_ge(K_.sem["dve"], b_dve + mc + 1)
                            if mc >= 2:
                                tn.wait_ge(K_.sem["act"], b_act + 2 * mc - 2)
                            cpre = t["f0_cp0"] if mc % 2 == 0 else t["f0_cp1"]
                            K_.inc(tn.matmul(psH[:], t["sb_h128bf"][:], cpre[:],
                                             start=True, stop=True), "pe", 1)

                    @blk.scalar
                    def _(s):
                        for mc in range(NH0):
                            psG = t["ps2"] if mc % 2 == 0 else t["ps3"]
                            psH = t["ps4"] if mc % 2 == 0 else t["ps5"]
                            sil = t["f0_sil0"] if mc % 2 == 0 else t["f0_sil1"]
                            s.wait_ge(K_.sem["pe"], b_pe + 2 * mc + 1)
                            if mc >= 2:
                                s.wait_ge(K_.sem["dve"], b_dve + mc - 1)
                            K_.inc(s.activation(sil[:], psG[:], AF.Silu,
                                                scale=t["sb_wscg_s"][:, mc:mc + 1]),
                                   "act", 1)
                            s.wait_ge(K_.sem["pe"], b_pe + 2 * mc + 2)
                            K_.inc(s.activation(t["f0_c"][:, mc * TQ:(mc + 1) * TQ],
                                                psH[:], AF.Copy, scale=RS8192),
                                   "act", 1)

                    @blk.vector
                    def _(v):
                        for mc in range(NH0):
                            psU = t["ps0"] if mc % 2 == 0 else t["ps1"]
                            sil = t["f0_sil0"] if mc % 2 == 0 else t["f0_sil1"]
                            cpre = t["f0_cp0"] if mc % 2 == 0 else t["f0_cp1"]
                            v.wait_ge(K_.sem["act"], b_act + 2 * mc + 1)
                            if mc >= 2:
                                v.wait_ge(K_.sem["pe"], b_pe + 2 * mc - 2)
                            K_.inc(v.scalar_tensor_tensor(
                                cpre[:], sil[:], t["sb_wscu_su"][:, mc:mc + 1],
                                psU[:], ALU.mult, ALU.mult), "dve", 1)

            K_.fence()
            with ExitStack() as es_c:
                sbuf(K_, es_c, t, "f0_wd", [128, 8 * TQ], BF16)
                sbuf(K_, es_c, t, "f0_xold", [128, 16 * TQ], F32)
                sbuf(K_, es_c, t, "f0_xnew", [128, 16 * TQ], F32)
                _emit_down(K_, t, x_in, x_out, tq, "f0_c", NH0, "f0_wd",
                           "wbf_down", "f0_xold", "f0_xnew",
                           scale_ap="sb_wscdp_h")


def _emit_down(K_, t, x_in, x_out, tq, cname, nkh, wdname, wbfname,
               xoldname, xnewname, scale_ap=None, c_stride=TQ, c_off=0):
    nc = K_.nc
    with nc.Block() as blk:
        b_pe = K_.c["pe"]
        b_dve = K_.c["dve"]
        ld = {}
        xld = []
        st_ev = []

        @blk.gpsimd
        def _(g):
            for mc in range(NCH):
                xld.append(K_.inc(g.dma_start(
                    t[xoldname][:, mc * TQ:(mc + 1) * TQ],
                    x_in[mc * 128:(mc + 1) * 128, tq * TQ:(tq + 1) * TQ]),
                    "gd", 16))

        @blk.sync
        def _(sy):
            for mq in range(4):
                for kh in range(nkh):
                    gi = mq * nkh + kh
                    if gi >= 8:
                        sy.wait_ge(K_.sem["pe"], b_pe + gi - 7)
                    ld[gi] = K_.inc(sy.dma_start_transpose(
                        t[wdname][:, (gi % 8) * TQ:(gi % 8 + 1) * TQ],
                        t[wbfname][mq * 512:(mq + 1) * 512,
                                   kh * 128:(kh + 1) * 128]), "sd", 16)

        @blk.tensor
        def _(tn):
            for mq in range(4):
                if mq >= 1:
                    tn.wait_ge(K_.sem["dve"], b_dve + 4 * mq)
                for kh in range(nkh):
                    gi = mq * nkh + kh
                    tn.wait_ge(K_.sem["sd"], ld[gi])
                    for mj in range(4):
                        ins = tn.matmul(
                            t["ps%d" % mj][:],
                            t[wdname][:, (gi % 8) * TQ + mj * 128:
                                      (gi % 8) * TQ + (mj + 1) * 128],
                            t[cname][:, kh * c_stride + c_off:
                                     kh * c_stride + c_off + TQ],
                            start=(kh == 0), stop=(kh == nkh - 1))
                    K_.inc(ins, "pe", 1)

        @blk.vector
        def _(v):
            for mq in range(4):
                v.wait_ge(K_.sem["pe"], b_pe + (mq + 1) * nkh)
                for mj in range(4):
                    mc = mq * 4 + mj
                    v.wait_ge(K_.sem["gd"], xld[mc])
                    sc = (t[scale_ap][:, mc:mc + 1] if scale_ap else 0.5)
                    st_ev.append(K_.inc(v.scalar_tensor_tensor(
                        t[xnewname][:, mc * TQ:(mc + 1) * TQ],
                        t["ps%d" % mj][:], sc,
                        t[xoldname][:, mc * TQ:(mc + 1) * TQ],
                        ALU.mult, ALU.add), "dve", 1))

        @blk.gpsimd
        def _(g):
            for mc in range(NCH):
                g.wait_ge(K_.sem["dve"], st_ev[mc])
                K_.inc(g.dma_start(
                    x_out[mc * 128:(mc + 1) * 128, tq * TQ:(tq + 1) * TQ],
                    t[xnewname][:, mc * TQ:(mc + 1) * TQ]), "gd", 16)




def emit_ffn1(K_, t, x_in, x_out):
    nc = K_.nc
    NMS = NH1 // 4
    TP = 1024
    for tp in range(2):
        with ExitStack() as es_tq:
            sbuf(K_, es_tq, t, "f1_y", [128, 16 * TP], BF16)
            sbuf(K_, es_tq, t, "f1_c", [128, NH1 * TP], BF16)
            K_.fence()
            with ExitStack() as es_a:
                alloc_ln_ws(K_, es_a, t)
                with nc.Block() as blk:
                    emit_ln_tq(K_, blk, t, x_in, 2 * tp, t["f1_y"], TP, 0)
                with nc.Block() as blk:
                    emit_ln_tq(K_, blk, t, x_in, 2 * tp + 1, t["f1_y"], TP, TQ)

            K_.fence()
            with ExitStack() as es_b:
                sbuf(K_, es_b, t, "f1_wA", [128, 16 * TP], BF16)
                sbuf(K_, es_b, t, "f1_wB", [128, 16 * TP], BF16)
                sbuf(K_, es_b, t, "f1_sig0", [128, TP], F32)
                sbuf(K_, es_b, t, "f1_sig1", [128, TP], F32)
                ld1 = {}
                with nc.Block() as blk:
                    b_pe = K_.c["pe"]
                    b_act = K_.c["act"]
                    b_dve = K_.c["dve"]

                    @blk.sync
                    def _(sy):
                        for ms in range(NMS):
                            if ms >= 2:
                                sy.wait_ge(K_.sem["pe"], b_pe + 4 * (ms - 1) + 1)
                            wb = t["f1_wA"] if ms % 2 == 0 else t["f1_wB"]
                            for kc in range(NCH):
                                K_.inc(sy.dma_start_transpose(
                                    wb[:, kc * TP:kc * TP + 512],
                                    t["wbf_w1"][ms * 512:(ms + 1) * 512,
                                                kc * 128:(kc + 1) * 128]),
                                    "sd", 16)
                                ld1[ms] = K_.inc(sy.dma_start_transpose(
                                    wb[:, kc * TP + 512:kc * TP + 1024],
                                    t["wbf_w2"][ms * 512:(ms + 1) * 512,
                                                kc * 128:(kc + 1) * 128]),
                                    "sd", 16)

                    @blk.tensor
                    def _(tn):
                        for mc in range(NH1):
                            ms, mj = mc // 4, mc % 4
                            wb = t["f1_wA"] if ms % 2 == 0 else t["f1_wB"]
                            tn.wait_ge(K_.sem["sd"], ld1[ms])
                            if mc >= 1:
                                tn.wait_ge(K_.sem["dve"], b_dve + mc)
                                tn.wait_ge(K_.sem["act"], b_act + mc)
                            for kc in range(NCH):
                                for tt in range(2):
                                    tn.matmul(t["ps%d" % tt][:],
                                              wb[:, kc * TP + mj * 128:
                                                 kc * TP + (mj + 1) * 128],
                                              t["f1_y"][:, kc * TP + tt * TQ:
                                                        kc * TP + (tt + 1) * TQ],
                                              start=(kc == 0),
                                              stop=(kc == NCH - 1))
                            for kc in range(NCH):
                                for tt in range(2):
                                    ins = tn.matmul(
                                        t["ps%d" % (2 + tt)][:],
                                        wb[:, kc * TP + 512 + mj * 128:
                                           kc * TP + 512 + (mj + 1) * 128],
                                        t["f1_y"][:, kc * TP + tt * TQ:
                                                  kc * TP + (tt + 1) * TQ],
                                        start=(kc == 0), stop=(kc == NCH - 1))
                            K_.inc(ins, "pe", 1)

                    @blk.scalar
                    def _(s):
                        for mc in range(NH1):
                            sig = t["f1_sig0"] if mc % 2 == 0 else t["f1_sig1"]
                            s.wait_ge(K_.sem["pe"], b_pe + mc + 1)
                            if mc >= 2:
                                s.wait_ge(K_.sem["dve"], b_dve + mc - 1)
                            s.activation(sig[:, 0:TQ], t["ps0"][:], AF.Sigmoid)
                            K_.inc(s.activation(sig[:, TQ:TP], t["ps1"][:],
                                                AF.Sigmoid), "act", 1)

                    @blk.vector
                    def _(v):
                        for mc in range(NH1):
                            sig = t["f1_sig0"] if mc % 2 == 0 else t["f1_sig1"]
                            v.wait_ge(K_.sem["act"], b_act + mc + 1)
                            v.tensor_mul(t["f1_c"][:, mc * TP:mc * TP + TQ],
                                         sig[:, 0:TQ], t["ps2"][:])
                            K_.inc(v.tensor_mul(
                                t["f1_c"][:, mc * TP + TQ:(mc + 1) * TP],
                                sig[:, TQ:TP], t["ps3"][:]), "dve", 1)

            K_.fence()
            with ExitStack() as es_c:
                sbuf(K_, es_c, t, "f1_wd", [128, 8 * TQ], BF16)
                sbuf(K_, es_c, t, "f1_xold", [128, 16 * TQ], F32)
                sbuf(K_, es_c, t, "f1_xnew", [128, 16 * TQ], F32)
                for tt in range(2):
                    _emit_down(K_, t, x_in, x_out, 2 * tp + tt, "f1_c", NH1,
                               "f1_wd", "wbf_w3", "f1_xold", "f1_xnew",
                               scale_ap=None, c_stride=TP, c_off=tt * TQ)
                    K_.fence()




def emit_scan(K_, t, x_in, x_out):
    nc = K_.nc
    K_.fence()
    with ExitStack() as es_s:
        sbuf(K_, es_s, t, "sc_lnb", [128, 16 * T], BF16)
        with ExitStack() as es_a:
            alloc_ln_ws(K_, es_a, t)
            for tq in range(NTQ):
                K_.fence()
                with nc.Block() as blk:
                    emit_ln_tq(K_, blk, t, x_in, tq, t["sc_lnb"], T, tq * TQ)

        K_.fence()
        with ExitStack() as es_b:
            sbuf(K_, es_b, t, "sc_w0", [128, 32 * 128], BF16)
            sbuf(K_, es_b, t, "sc_w1", [128, 32 * 128], BF16)
            sbuf(K_, es_b, t, "sc_q", [128, T], F32)
            sbuf(K_, es_b, t, "sc_g", [128, T], F32)
            sbuf(K_, es_b, t, "sc_h", [128, T], F32)
            sbuf(K_, es_b, t, "sc_xold", [128, T], F32)
            scld = {}
            sc_st = {}
            with nc.Block() as blk:
                @blk.sync
                def _(sy):
                    for pi, m in ((0, 0), (1, 16)):
                        for kc in range(NCH):
                            scld[0] = K_.inc(sy.dma_start_transpose(
                                t["sc_w0"][:, (pi * NCH + kc) * 128:
                                           (pi * NCH + kc + 1) * 128],
                                t["wbf_l1"][m * 128:(m + 1) * 128,
                                            kc * 128:(kc + 1) * 128]), "sd", 16)

            for p in range(NCH):
                with nc.Block() as blk:
                    wl = t["sc_w0"] if p % 2 == 0 else t["sc_w1"]
                    wln = t["sc_w1"] if p % 2 == 0 else t["sc_w0"]

                    @blk.sync
                    def _(sy):
                        if p + 1 < NCH:
                            for pi, m in ((0, p + 1), (1, p + 17)):
                                for kc in range(NCH):
                                    scld[p + 1] = K_.inc(sy.dma_start_transpose(
                                        wln[:, (pi * NCH + kc) * 128:
                                            (pi * NCH + kc + 1) * 128],
                                        t["wbf_l1"][m * 128:(m + 1) * 128,
                                                    kc * 128:(kc + 1) * 128]),
                                        "sd", 16)

                    xld = []

                    @blk.gpsimd
                    def _(g):
                        xld.append(K_.inc(
                            g.dma_start(t["sc_xold"][:],
                                        x_in[p * 128:(p + 1) * 128, :]),
                            "gd", 16))

                    @blk.tensor
                    def _(tn):
                        tn.wait_ge(K_.sem["sd"], scld[p])
                        for kc in range(NCH):
                            for tt in range(NTQ):
                                ins = tn.matmul(
                                    t["ps%d" % tt][:],
                                    wl[:, kc * 128:(kc + 1) * 128],
                                    t["sc_lnb"][:, kc * T + tt * TQ:
                                                kc * T + (tt + 1) * TQ],
                                    start=(kc == 0), stop=(kc == NCH - 1))
                        K_.inc(ins, "pe", 1)

                    @blk.vector
                    def _(v):
                        v.wait_ge(K_.sem["pe"], K_.c["pe"])
                        for tt in range(NTQ):
                            ins = v.tensor_single_scalar(
                                t["sc_q"][:, tt * TQ:(tt + 1) * TQ],
                                t["ps%d" % tt][:], 1e-20, ALU.max)
                        K_.inc(ins, "dve", 1)

                    @blk.tensor
                    def _(tn):
                        tn.wait_ge(K_.sem["dve"], K_.c["dve"])
                        for kc in range(NCH):
                            for tt in range(NTQ):
                                ins = tn.matmul(
                                    t["ps%d" % tt][:],
                                    wl[:, (NCH + kc) * 128:(NCH + kc + 1) * 128],
                                    t["sc_lnb"][:, kc * T + tt * TQ:
                                                kc * T + (tt + 1) * TQ],
                                    start=(kc == 0), stop=(kc == NCH - 1))
                        K_.inc(ins, "pe", 1)

                    @blk.scalar
                    def _(s):
                        s.wait_ge(K_.sem["pe"], K_.c["pe"])
                        for tt in range(NTQ):
                            ins = s.activation(t["sc_g"][:, tt * TQ:(tt + 1) * TQ],
                                               t["ps%d" % tt][:], AF.Sigmoid)
                        K_.inc(ins, "act", 1)

                    @blk.vector
                    def _(v):
                        v.wait_ge(K_.sem["act"], K_.c["act"])
                        v.tensor_single_scalar(t["sc_g"][:], t["sc_g"][:],
                                               1e-20, ALU.max)
                        if p >= 1:
                            v.wait_ge(K_.sem["gd"], sc_st[p - 1])
                        v.tensor_tensor_scan(t["sc_h"][:], t["sc_g"][:],
                                             t["sc_q"][:], 0.0, ALU.mult, ALU.add)
                        v.wait_ge(K_.sem["gd"], xld[0])
                        K_.inc(v.tensor_add(t["sc_h"][:], t["sc_h"][:],
                                            t["sc_xold"][:]), "dve", 1)

                    @blk.gpsimd
                    def _(g):
                        g.wait_ge(K_.sem["dve"], K_.c["dve"])
                        sc_st[p] = K_.inc(
                            g.dma_start(x_out[p * 128:(p + 1) * 128, :],
                                        t["sc_h"][:]), "gd", 16)




def _emit_rope(K_, v, t, src, dst):
    for h0 in (0, T // 2):
        tmp, tmp2 = t["at_rt1"], t["at_rt2"]
        sl = slice(h0, h0 + T // 2)
        v.tensor_mul(tmp[:], src[:, sl], t["sb_cos"][:, sl])
        v.tensor_mul(tmp2[0:64, :], src[64:128, sl], t["sb_sin"][64:128, sl])
        v.tensor_mul(tmp2[64:128, :], src[0:64, sl], t["sb_sin"][0:64, sl])
        v.tensor_sub(dst[0:64, sl], tmp[0:64, :], tmp2[0:64, :])
        ins = v.tensor_add(dst[64:128, sl], tmp[64:128, :], tmp2[64:128, :])
    return ins


def emit_att(K_, t, x_in, x_out):
    nc = K_.nc
    K_.fence()
    with ExitStack() as es_att:
        sbuf(K_, es_att, t, "at_kbf", [128, 4 * T], BF16)
        sbuf(K_, es_att, t, "at_vtok", [128, 4 * T], BF16)
        sbuf(K_, es_att, t, "at_qbf", [128, T], BF16)
        sbuf(K_, es_att, t, "at_qbf1", [128, T], BF16)
        sbuf(K_, es_att, t, "at_mx", [128, 64], F32)
        sbuf(K_, es_att, t, "at_sm", [128, 64], F32)
        with ExitStack() as es_1:
            sbuf(K_, es_1, t, "at_lnb", [128, 16 * T], BF16)
            with ExitStack() as es_a:
                alloc_ln_ws(K_, es_a, t)
                for tq in range(NTQ):
                    K_.fence()
                    with nc.Block() as blk:
                        emit_ln_tq(K_, blk, t, x_in, tq, t["at_lnb"], T, tq * TQ)

            K_.fence()
            sbuf(K_, es_1, t, "at_wkv", [128, 16 * TQ], BF16)
            sbuf(K_, es_1, t, "at_f32", [128, T], F32)
            sbuf(K_, es_1, t, "at_rt1", [128, T // 2], F32)
            sbuf(K_, es_1, t, "at_rt2", [128, T // 2], F32)
            sbuf(K_, es_1, t, "at_vbf", [128, T], BF16)
            sbuf(K_, es_1, t, "at_wq0", [128, 16 * 128], BF16)
            sbuf(K_, es_1, t, "at_wq1", [128, 16 * 128], BF16)

            ldkv = [0]
            with nc.Block() as blk:
                @blk.sync
                def _(sy):
                    for kc in range(NCH):
                        ldkv[0] = K_.inc(sy.dma_start_transpose(
                            t["at_wkv"][:, kc * TQ:(kc + 1) * TQ],
                            t["wbf_k"][:, kc * 128:(kc + 1) * 128]), "sd", 16)

            for h in range(4):
                with nc.Block() as blk:
                    @blk.tensor
                    def _(tn):
                        tn.wait_ge(K_.sem["sd"], ldkv[0])
                        for kc in range(NCH):
                            for tt in range(NTQ):
                                ins = tn.matmul(
                                    t["ps%d" % tt][:],
                                    t["at_wkv"][:, kc * TQ + h * 128:
                                                kc * TQ + (h + 1) * 128],
                                    t["at_lnb"][:, kc * T + tt * TQ:
                                                kc * T + (tt + 1) * TQ],
                                    start=(kc == 0), stop=(kc == NCH - 1))
                        K_.inc(ins, "pe", 1)

                    @blk.vector
                    def _(v):
                        v.wait_ge(K_.sem["pe"], K_.c["pe"])
                        for tt in range(NTQ):
                            v.tensor_scalar(t["at_f32"][:, tt * TQ:(tt + 1) * TQ],
                                            t["ps%d" % tt][:],
                                            t["sb_bkv"][:, h:h + 1], None,
                                            ALU.add)
                        K_.inc(_emit_rope(K_, v, t, t["at_f32"],
                                          t["at_kbf"][:, h * T:(h + 1) * T]),
                               "dve", 1)

            K_.fence()
            with nc.Block() as blk:
                @blk.sync
                def _(sy):
                    for kc in range(NCH):
                        ldkv[0] = K_.inc(sy.dma_start_transpose(
                            t["at_wkv"][:, kc * TQ:(kc + 1) * TQ],
                            t["wbf_v"][:, kc * 128:(kc + 1) * 128]), "sd", 16)

            vtev = {}
            for h in range(4):
                with nc.Block() as blk:
                    @blk.tensor
                    def _(tn):
                        tn.wait_ge(K_.sem["sd"], ldkv[0])
                        for kc in range(NCH):
                            for tt in range(NTQ):
                                ins = tn.matmul(
                                    t["ps%d" % tt][:],
                                    t["at_wkv"][:, kc * TQ + h * 128:
                                                kc * TQ + (h + 1) * 128],
                                    t["at_lnb"][:, kc * T + tt * TQ:
                                                kc * T + (tt + 1) * TQ],
                                    start=(kc == 0), stop=(kc == NCH - 1))
                        K_.inc(ins, "pe", 1)

                    @blk.vector
                    def _(v):
                        v.wait_ge(K_.sem["pe"], K_.c["pe"])
                        for tt in range(NTQ):
                            ins = v.tensor_scalar(
                                t["at_vbf"][:, tt * TQ:(tt + 1) * TQ],
                                t["ps%d" % tt][:],
                                t["sb_bvv"][:, h:h + 1], None, ALU.add)
                        K_.inc(ins, "dve", 1)

                    for w0 in range(0, NCH, 4):
                        tev = [0]

                        @blk.tensor
                        def _(tn, w0=w0, h=h):
                            psb = t["psbfA"] if (w0 // 4) % 2 == 0 else t["psbfB"]
                            if w0 == 0:
                                tn.wait_ge(K_.sem["dve"], K_.c["dve"])
                            if w0 >= 8:
                                tn.wait_ge(K_.sem["act"], vtev[(h, w0 - 5)])
                            elif h > 0:
                                tn.wait_ge(K_.sem["act"], vtev[(h - 1, w0 + 11)])
                            for kb in range(w0, w0 + 4):
                                ins = tn.transpose(
                                    psb[:, (kb % 4) * 128:(kb % 4 + 1) * 128],
                                    t["at_vbf"][:, kb * 128:(kb + 1) * 128],
                                    t["sb_identbf"][:])
                            tev[0] = K_.inc(ins, "pe", 1)

                        @blk.scalar
                        def _(s, w0=w0, h=h):
                            psb = t["psbfA"] if (w0 // 4) % 2 == 0 else t["psbfB"]
                            s.wait_ge(K_.sem["pe"], tev[0])
                            for kb in range(w0, w0 + 4):
                                vtev[(h, kb)] = K_.inc(s.copy(
                                    t["at_vtok"][:, h * T + kb * 128:
                                                 h * T + (kb + 1) * 128],
                                    psb[:, (kb % 4) * 128:(kb % 4 + 1) * 128]),
                                    "act", 1)

            K_.fence()
            ldq = {}
            with nc.Block() as blk:
                @blk.sync
                def _(sy):
                    for kc in range(NCH):
                        ldq[0] = K_.inc(sy.dma_start_transpose(
                            t["at_wq0"][:, kc * 128:(kc + 1) * 128],
                            t["wbf_q"][0:128, kc * 128:(kc + 1) * 128]),
                            "sd", 16)

            for qh in range(16):
                with nc.Block() as blk:
                    wq = t["at_wq0"] if qh % 2 == 0 else t["at_wq1"]
                    wqn = t["at_wq1"] if qh % 2 == 0 else t["at_wq0"]

                    @blk.sync
                    def _(sy):
                        if qh + 1 < 16:
                            for kc in range(NCH):
                                ldq[qh + 1] = K_.inc(sy.dma_start_transpose(
                                    wqn[:, kc * 128:(kc + 1) * 128],
                                    t["wbf_q"][(qh + 1) * 128:(qh + 2) * 128,
                                               kc * 128:(kc + 1) * 128]),
                                    "sd", 16)

                    @blk.tensor
                    def _(tn):
                        tn.wait_ge(K_.sem["sd"], ldq[qh])
                        for kc in range(NCH):
                            for tt in range(NTQ):
                                ins = tn.matmul(
                                    t["ps%d" % tt][:],
                                    wq[:, kc * 128:(kc + 1) * 128],
                                    t["at_lnb"][:, kc * T + tt * TQ:
                                                kc * T + (tt + 1) * TQ],
                                    start=(kc == 0), stop=(kc == NCH - 1))
                        K_.inc(ins, "pe", 1)

                    @blk.vector
                    def _(v):
                        v.wait_ge(K_.sem["pe"], K_.c["pe"])
                        for tt in range(NTQ):
                            v.tensor_scalar(t["at_f32"][:, tt * TQ:(tt + 1) * TQ],
                                            t["ps%d" % tt][:], RS128,
                                            t["sb_bq_s"][:, qh:qh + 1],
                                            ALU.mult, ALU.add)
                        K_.inc(_emit_rope(K_, v, t, t["at_f32"], t["at_qbf"]),
                               "dve", 1)

                    @blk.gpsimd
                    def _(g):
                        g.wait_ge(K_.sem["dve"], K_.c["dve"])
                        K_.inc(g.dma_start(t["xq_sc"][qh], t["at_qbf"][:]),
                               "gd", 16)

        if t.get("att_dbg"):
            with nc.Block() as blk:
                @blk.gpsimd
                def _(g):
                    for h in range(4):
                        K_.inc(g.dma_start(t["kv_sc"][h],
                                           t["at_kbf"][:, h * T:(h + 1) * T]),
                               "gd", 16)
                        K_.inc(g.dma_start(t["kv_sc"][4 + h],
                                           t["at_vtok"][:, h * T:(h + 1) * T]),
                               "gd", 16)

        K_.fence()
        with ExitStack() as es_2:
            sbuf(K_, es_2, t, "at_pf", [128, T], F32)
            sbuf(K_, es_2, t, "at_pbf", [128, T], BF16)
            sbuf(K_, es_2, t, "at_ptb", [128, 8 * 128], BF16)
            sbuf(K_, es_2, t, "at_ost", [128, T], BF16)
            ost_st = [0]

            qld = {}
            oev = {}
            eevs = {}
            twlast = {}
            owlast = {}
            ptlast = {}
            sevlast = {}
            with nc.Block() as blk:
                for qh in range(16):
                    h = qh // 4
                    qb_cur = t["at_qbf"] if qh % 2 == 0 else t["at_qbf1"]
                    qb_nxt = t["at_qbf1"] if qh % 2 == 0 else t["at_qbf"]

                    @blk.gpsimd
                    def _(g, qh=qh, qb_nxt=qb_nxt):
                        if qh == 0:
                            qld[0] = K_.inc(g.dma_start(t["at_qbf"][:],
                                                        t["xq_sc"][0]), "gd", 16)
                        if qh + 1 < 16:
                            if qh >= 1:
                                g.wait_ge(K_.sem["pe"], sevlast[qh - 1])
                            qld[qh + 1] = K_.inc(g.dma_start(
                                qb_nxt[:], t["xq_sc"][qh + 1]), "gd", 16)

                    for qt in range(NCH):
                        gq = qh * NCH + qt
                        kmax = (qt + 1) * 128
                        nsp = (kmax + TQ - 1) // TQ
                        sb_ = 32 * (gq % 2)
                        sev = [0]
                        mev = [0]
                        eev = [0]
                        pev = [0]
                        smev = [0]
                        rcev = [0]

                        @blk.tensor
                        def _(tn, qh=qh, qt=qt, gq=gq, kmax=kmax, nsp=nsp,
                              h=h, qb_cur=qb_cur):
                            if qt == 0:
                                tn.wait_ge(K_.sem["gd"], qld[qh])
                            if gq >= 1:
                                tn.wait_ge(K_.sem["act"], eevs[gq - 1])
                            for ns in range(nsp):
                                n0 = ns * TQ
                                n1 = min(kmax, n0 + TQ)
                                ins = tn.matmul(
                                    t["ps%d" % ns][:, 0:n1 - n0],
                                    qb_cur[:, qt * 128:(qt + 1) * 128],
                                    t["at_kbf"][:, h * T + n0:h * T + n1],
                                    start=True, stop=True)
                            sev[0] = sevlast[qh] = K_.inc(ins, "pe", 1)

                        @blk.vector
                        def _(v):
                            v.wait_ge(K_.sem["pe"], sev[0])
                            ns_l = (kmax - 1) // TQ
                            off = qt * 128 - ns_l * TQ
                            v.tensor_add(t["ps%d" % ns_l][:, off:off + 128],
                                         t["ps%d" % ns_l][:, off:off + 128],
                                         t["sb_maskc"][:])
                            for ns in range(nsp):
                                n0 = ns * TQ
                                n1 = min(kmax, n0 + TQ)
                                v.reduce_max(t["ps5"][:, sb_ + ns:sb_ + ns + 1],
                                             t["ps%d" % ns][:, 0:n1 - n0], AX.X)
                            v.reduce_max(t["ps5"][:, sb_ + 8:sb_ + 9],
                                         t["ps5"][:, sb_:sb_ + nsp], AX.X)
                            v.tensor_scalar_mul(t["ps5"][:, sb_ + 10:sb_ + 11],
                                                t["ps5"][:, sb_ + 8:sb_ + 9], -1.0)
                            mev[0] = K_.inc(v.tensor_copy(
                                t["at_mx"][:, sb_:sb_ + 32],
                                t["ps5"][:, sb_:sb_ + 32]), "dve", 1)

                        @blk.scalar
                        def _(s):
                            s.wait_ge(K_.sem["dve"], mev[0])
                            for ns in range(nsp):
                                n0 = ns * TQ
                                n1 = min(kmax, n0 + TQ)
                                ins = s.activation(
                                    t["at_pf"][:, n0:n1],
                                    t["ps%d" % ns][:, 0:n1 - n0], AF.Exp,
                                    bias=t["at_mx"][:, sb_ + 10:sb_ + 11])
                            eev[0] = eevs[gq] = K_.inc(ins, "act", 1)

                        @blk.vector
                        def _(v):
                            v.wait_ge(K_.sem["act"], eev[0])
                            ins = v.reduce_sum(t["ps5"][:, sb_ + 12:sb_ + 13],
                                               t["at_pf"][:, 0:kmax], AX.X)
                            smev[0] = K_.inc(ins, "dve", 1)

                        @blk.scalar
                        def _(s):
                            s.wait_ge(K_.sem["dve"], smev[0])
                            s.activation(t["ps5"][:, sb_ + 14:sb_ + 15],
                                         t["ps5"][:, sb_ + 12:sb_ + 13], AF.Ln)
                            s.activation(t["ps5"][:, sb_ + 16:sb_ + 17],
                                         t["ps5"][:, sb_ + 14:sb_ + 15],
                                         AF.Exp, scale=-1.0)
                            rcev[0] = K_.inc(s.copy(t["at_sm"][:, sb_:sb_ + 32],
                                                    t["ps5"][:, sb_:sb_ + 32]),
                                             "act", 1)

                        @blk.vector
                        def _(v):
                            v.wait_ge(K_.sem["act"], rcev[0])
                            if gq >= 1 and (gq - 1) in twlast:
                                v.wait_ge(K_.sem["pe"], twlast[gq - 1])
                            pev[0] = K_.inc(v.tensor_scalar(
                                t["at_pbf"][:, 0:kmax], t["at_pf"][:, 0:kmax],
                                t["at_sm"][:, sb_ + 16:sb_ + 17], None,
                                ALU.mult), "dve", 1)

                        if t.get("att_dbg") and qh == 0:
                            @blk.gpsimd
                            def _(g, qt=qt, kmax=kmax):
                                g.wait_ge(K_.sem["dve"], pev[0])
                                K_.inc(g.dma_start(
                                    t["p_sc"][qt, :, 0:kmax],
                                    t["at_pbf"][:, 0:kmax]), "gd", 16)

                        nkb = qt + 1
                        opev = [0]
                        owev = {}
                        ptev = {}
                        for w0 in range(0, nkb, 4):
                            w1 = min(nkb, w0 + 4)
                            twev = [0]

                            @blk.tensor
                            def _(tn, w0=w0, w1=w1, qt=qt, gq=gq, ptev=ptev):
                                psb = (t["psbfA"] if (w0 // 4) % 2 == 0
                                       else t["psbfB"])
                                tn.wait_ge(K_.sem["dve"], pev[0])
                                if w0 >= 8:
                                    tn.wait_ge(K_.sem["act"], ptev[w0 - 5])
                                elif gq >= 1 and (gq - 1) in ptlast:
                                    tn.wait_ge(K_.sem["act"], ptlast[gq - 1])
                                for kb in range(w0, w1):
                                    ins = tn.transpose(
                                        psb[:, (kb % 4) * 128:(kb % 4 + 1) * 128],
                                        t["at_pbf"][:, kb * 128:(kb + 1) * 128],
                                        t["sb_identbf"][:])
                                twev[0] = twlast[gq] = K_.inc(ins, "pe", 1)

                            @blk.scalar
                            def _(s, w0=w0, w1=w1, qt=qt, gq=gq, ptev=ptev):
                                psb = (t["psbfA"] if (w0 // 4) % 2 == 0
                                       else t["psbfB"])
                                s.wait_ge(K_.sem["pe"], twev[0])
                                if w0 >= 8:
                                    s.wait_ge(K_.sem["pe"], owev[w0 - 8])
                                elif gq >= 1 and (gq - 1) in owlast:
                                    s.wait_ge(K_.sem["pe"], owlast[gq - 1])
                                for kb in range(w0, w1):
                                    ptev[kb] = K_.inc(s.copy(
                                        t["at_ptb"][:, (kb % 8) * 128:
                                                    (kb % 8 + 1) * 128],
                                        psb[:, (kb % 4) * 128:
                                            (kb % 4 + 1) * 128]), "act", 1)
                                ptlast[gq] = ptev[w1 - 1]

                            @blk.tensor
                            def _(tn, w0=w0, w1=w1, qt=qt, gq=gq, ptev=ptev):
                                if w0 == 0 and gq >= 1:
                                    tn.wait_ge(K_.sem["act"], oev[gq - 1])
                                for kb in range(w0, w1):
                                    tn.wait_ge(K_.sem["act"], ptev[kb])
                                    ins = tn.matmul(
                                        t["ps4"][:, 0:128],
                                        t["at_vtok"][:, h * T + kb * 128:
                                                     h * T + (kb + 1) * 128],
                                        t["at_ptb"][:, (kb % 8) * 128:
                                                    (kb % 8 + 1) * 128],
                                        start=(kb == 0), stop=(kb == nkb - 1))
                                owev[w0] = owlast[gq] = opev[0] = K_.inc(ins, "pe", 1)

                        @blk.scalar
                        def _(s, qt=qt, gq=gq):
                            s.wait_ge(K_.sem["pe"], opev[0])
                            if qt == 0:
                                s.wait_ge(K_.sem["gd"], ost_st[0])
                            oev[gq] = K_.inc(s.copy(
                                t["at_ost"][:, qt * 128:(qt + 1) * 128],
                                t["ps4"][:, 0:128]), "act", 1)

                    @blk.gpsimd
                    def _(g, qh=qh):
                        g.wait_ge(K_.sem["act"], oev[qh * NCH + NCH - 1])
                        ost_st[0] = K_.inc(g.dma_start(t["xo_sc"][qh],
                                                       t["at_ost"][:]), "gd", 16)

        K_.fence()
        with ExitStack() as es_3:
            sbuf(K_, es_3, t, "at_wo", [128, 16 * T], BF16)
            sbuf(K_, es_3, t, "at_otq", [128, 16 * TQ], BF16)
            sbuf(K_, es_3, t, "at_xo", [128, 4 * TQ], F32)
            sbuf(K_, es_3, t, "at_xn", [128, 4 * TQ], F32)
            ldo = [0]
            with nc.Block() as blk:
                @blk.sync
                def _(sy):
                    for ms in range(4):
                        for kc in range(NCH):
                            ldo[0] = K_.inc(sy.dma_start_transpose(
                                t["at_wo"][:, kc * T + ms * 512:
                                           kc * T + (ms + 1) * 512],
                                t["wbf_o"][ms * 512:(ms + 1) * 512,
                                           kc * 128:(kc + 1) * 128]), "sd", 16)

            for tq in range(NTQ):
                with nc.Block() as blk:
                    xld = {}
                    old = {}
                    stv = {}
                    zpe = {}
                    b_dve_wo = K_.c["dve"]

                    sd0_wo = K_.c["sd"]

                    @blk.gpsimd
                    def _(g):
                        for kc in range(NCH):
                            old[kc] = K_.inc(g.dma_start(
                                t["at_otq"][:, kc * TQ:(kc + 1) * TQ],
                                t["xo_sc"][kc, :, tq * TQ:(tq + 1) * TQ]),
                                "gd", 16)
                        for mc in range(NCH):
                            if mc >= 4:
                                g.wait_ge(K_.sem["dve"], b_dve_wo + mc - 3)
                            xld[mc] = K_.inc(g.dma_start(
                                t["at_xo"][:, (mc % 4) * TQ:(mc % 4 + 1) * TQ],
                                x_in[mc * 128:(mc + 1) * 128,
                                     tq * TQ:(tq + 1) * TQ]), "gd", 16)

                    @blk.tensor
                    def _(tn):
                        tn.wait_ge(K_.sem["sd"], ldo[0])
                        tn.wait_ge(K_.sem["gd"], old[NCH - 1])
                        for mc in range(NCH):
                            psZ = t["ps0"] if mc % 2 == 0 else t["ps1"]
                            if mc >= 2:
                                tn.wait_ge(K_.sem["dve"], b_dve_wo + mc - 1)
                            for kc in range(NCH):
                                ins = tn.matmul(
                                    psZ[:],
                                    t["at_wo"][:, kc * T + mc * 128:
                                               kc * T + (mc + 1) * 128],
                                    t["at_otq"][:, kc * TQ:(kc + 1) * TQ],
                                    start=(kc == 0), stop=(kc == NCH - 1))
                            zpe[mc] = K_.inc(ins, "pe", 1)

                    stq = {}

                    @blk.vector
                    def _(v):
                        for mc in range(NCH):
                            psZ = t["ps0"] if mc % 2 == 0 else t["ps1"]
                            v.wait_ge(K_.sem["pe"], zpe[mc])
                            v.wait_ge(K_.sem["gd"], xld[mc])
                            if mc >= 4:
                                v.wait_ge(K_.sem["sd"], sd0_wo + 16 * (mc - 3))
                            stv[mc] = K_.inc(v.scalar_tensor_tensor(
                                t["at_xn"][:, (mc % 4) * TQ:(mc % 4 + 1) * TQ],
                                psZ[:], t["sb_bov"][:, mc:mc + 1],
                                t["at_xo"][:, (mc % 4) * TQ:(mc % 4 + 1) * TQ],
                                ALU.add, ALU.add), "dve", 1)

                    @blk.sync
                    def _(sy):
                        for mc in range(NCH):
                            sy.wait_ge(K_.sem["dve"], stv[mc])
                            stq[mc] = K_.inc(sy.dma_start(
                                x_out[mc * 128:(mc + 1) * 128,
                                      tq * TQ:(tq + 1) * TQ],
                                t["at_xn"][:, (mc % 4) * TQ:(mc % 4 + 1) * TQ]),
                                "sd", 16)




WSHAPES = {
    "wup": [H0, D], "wgate": [H0, D], "wdown": [D, H0],
    "wq": [D, D], "wk": [KV, D], "wv": [KV, D], "wo": [D, D],
    "wl1": [LQG, D], "w1": [H1, D], "w2": [H1, D], "w3": [D, H1],
}
BFSHAPES = {
    "wbf_up": [H0, D], "wbf_gate": [H0, D], "wbf_down": [D, H0],
    "wbf_q": [D, D], "wbf_k": [KV, D], "wbf_v": [KV, D], "wbf_o": [D, D],
    "wbf_l1": [LQG, D], "wbf_w1": [H1, D], "wbf_w2": [H1, D], "wbf_w3": [D, H1],
}
NAMEMAP = {"wup": "wbf_up", "wgate": "wbf_gate", "wdown": "wbf_down",
           "wq": "wbf_q", "wk": "wbf_k", "wv": "wbf_v", "wo": "wbf_o",
           "wl1": "wbf_l1", "w1": "wbf_w1", "w2": "wbf_w2", "w3": "wbf_w3"}
VECNAMES = ("bqv", "bkv", "bvv", "bov", "suug", "sudp", "wscu", "wscg", "wscdp")
CONSTNAMES = ("ident", "h128", "cosx", "sinx", "maskc", "ones_a", "ones_b")


def build_program(sub_ids):
    nc = bass.Bass("TRN2", target_bir_lowering=False)
    need = set(SUBLAYERS[i] for i in sub_ids)
    t = {}
    innames = ["x_in"]
    t["x_in"] = nc.dram_tensor("x_in", [D, T], F32, kind="ExternalInput").ap()
    t["y_out"] = nc.dram_tensor("y_out", [D, T], F32, kind="ExternalOutput").ap()
    wneed = []
    if "ffn0" in need:
        wneed += ["wup", "wgate", "wdown"]
    if "att" in need:
        wneed += ["wq", "wk", "wv", "wo"]
    if "scan" in need:
        wneed += ["wl1"]
    if "ffn1" in need:
        wneed += ["w1", "w2", "w3"]
    for w in wneed:
        t[w] = nc.dram_tensor(w, WSHAPES[w], F32, kind="ExternalInput").ap()
        innames.append(w)
    for name in VECNAMES:
        shape = {"bqv": [128, 16], "bkv": [128, 4], "bvv": [128, 4],
                 "bov": [128, 16], "suug": [128, 16], "sudp": [128, 64],
                 "wscu": [128, 64], "wscg": [128, 64], "wscdp": [128, 16]}[name]
        t[name] = nc.dram_tensor(name, shape, F32, kind="ExternalInput").ap()
        innames.append(name)
    for name in CONSTNAMES:
        shape = {"ident": [128, 128], "h128": [128, 128], "cosx": [128, T],
                 "sinx": [128, T], "maskc": [128, 128], "ones_a": [128, 1],
                 "ones_b": [1, 128]}[name]
        t[name] = nc.dram_tensor(name, shape, F32, kind="ExternalInput").ap()
        innames.append(name)
    qdbg = os.environ.get("QDBG", "0") == "1"
    if qdbg:
        t["qdbg"] = nc.dram_tensor("qdbg", [128, 1024], F32,
                                   kind="ExternalOutput").ap()
    for w in wneed:
        n = NAMEMAP[w]
        kk = "ExternalOutput" if (qdbg and n in ("wbf_w1", "wbf_w3")) else "Internal"
        t[n] = nc.dram_tensor(n, BFSHAPES[n], BF16, kind=kk).ap()
    if len(sub_ids) > 1:
        t["xA"] = nc.dram_tensor("xA", [D, T], F32, kind="Internal").ap()
        t["xB"] = nc.dram_tensor("xB", [D, T], F32, kind="Internal").ap()
    if "att" in need:
        dbg = os.environ.get("ATT_DBG", "0") == "1"
        kq = "ExternalOutput" if dbg else "Internal"
        t["att_dbg"] = dbg
        t["xq_sc"] = nc.dram_tensor("xq_sc", [16, 128, T], BF16, kind=kq).ap()
        t["xo_sc"] = nc.dram_tensor("xo_sc", [16, 128, T], BF16, kind=kq).ap()
        if dbg:
            t["kv_sc"] = nc.dram_tensor("kv_sc", [8, 128, T], BF16,
                                        kind="ExternalOutput").ap()
            t["p_sc"] = nc.dram_tensor("p_sc", [16, 128, T], BF16,
                                       kind="ExternalOutput").ap()

    with ExitStack() as es:
        K_ = K(nc, es)
        for i_ in range(6):
            t["ps%d" % i_] = es.enter_context(
                nc.psum_tensor("ps%d" % i_, [128, 512], F32))
        t["psbfA"] = es.enter_context(nc.psum_tensor("psbfA", [128, 512], BF16))
        t["psbfB"] = es.enter_context(nc.psum_tensor("psbfB", [128, 512], BF16))
        sbuf(K_, es, t, "sb_ones_a", [128, 1], F32)
        sbuf(K_, es, t, "sb_ones_b", [1, 128], F32)
        sbuf(K_, es, t, "sb_identbf", [128, 128], BF16)
        sbuf(K_, es, t, "sb_h128", [128, 128], F32)
        sbuf(K_, es, t, "sb_h128bf", [128, 128], BF16)
        sbuf(K_, es, t, "sb_h128su", [128, 16 * 128], BF16)
        sbuf(K_, es, t, "sb_maskc", [128, 128], F32)
        for name in VECNAMES:
            shp = {"bqv": 16, "bkv": 4, "bvv": 4, "bov": 16, "suug": 16,
                   "sudp": 64, "wscu": 64, "wscg": 64, "wscdp": 16}[name]
            sbuf(K_, es, t, "sb_" + name, [128, shp], F32)
        sbuf(K_, es, t, "sb_bq_s", [128, 16], F32)
        sbuf(K_, es, t, "sb_wscu_su", [128, 64], F32)
        sbuf(K_, es, t, "sb_wscg_s", [128, 64], F32)
        sbuf(K_, es, t, "sb_wscdp_h", [128, 16], F32)
        sbuf(K_, es, t, "sb_eps", [1, 1], F32)
        if "att" in need:
            sbuf(K_, es, t, "sb_cos", [128, T], BF16)
            sbuf(K_, es, t, "sb_sin", [128, T], BF16)

        p0_emit(K_, t, es, need)

        cur = t["x_in"]
        n = len(sub_ids)
        for i_, si in enumerate(sub_ids):
            last = (i_ == n - 1)
            nxt = t["y_out"] if last else (t["xA"] if i_ % 2 == 0 else t["xB"])
            kind = SUBLAYERS[si]
            if kind == "ffn0":
                emit_ffn0(K_, t, cur, nxt)
            elif kind == "ffn1":
                emit_ffn1(K_, t, cur, nxt)
            elif kind == "scan":
                emit_scan(K_, t, cur, nxt)
            elif kind == "att":
                emit_att(K_, t, cur, nxt)
            cur = nxt

        K_.fence()
    return nc, innames




def make_base_map(inputs):
    base = dict(host_consts())
    base["bqv"] = vec_chunks(inputs["L0_attn_bq"])
    base["bkv"] = vec_chunks(inputs["L0_attn_bk"])
    base["bvv"] = vec_chunks(inputs["L0_attn_bv"])
    base["bov"] = vec_chunks(inputs["L0_attn_bo"])
    base["suug"] = vec_chunks(inputs["L0_ffn_SU_ug"])
    base["sudp"] = vec_chunks(inputs["L0_ffn_SU_dp"])
    wsc = np.asarray(inputs["L0_ffn_Wscale_ug"], dtype=np.float32)
    base["wscu"] = vec_chunks(wsc[:H0])
    base["wscg"] = vec_chunks(wsc[H0:])
    base["wscdp"] = vec_chunks(inputs["L0_ffn_Wscale_dp"])
    for name, key in (("wup", "L0_ffn_W_up"), ("wgate", "L0_ffn_W_gate"),
                      ("wdown", "L0_ffn_W_down"), ("wq", "L0_attn_Wq"),
                      ("wk", "L0_attn_Wk"), ("wv", "L0_attn_Wv"),
                      ("wo", "L0_attn_Wo"), ("wl1", "L1_attn_W"),
                      ("w1", "L1_ffn_w1"), ("w2", "L1_ffn_w2"),
                      ("w3", "L1_ffn_w3")):
        base[name] = np.ascontiguousarray(np.asarray(inputs[key],
                                                     dtype=np.float32))
    return base


def kernel(**inputs):
    base = make_base_map(inputs)
    x = np.asarray(inputs["L0_attn_x"], dtype=np.float32)
    x_cur = [np.ascontiguousarray(x[c].T) for c in range(8)]
    groups = ([[i] for i in range(6)] if UNFUSED else [list(range(6))])
    for g in groups:
        nc, innames = build_program(g)
        in_maps = []
        for c in range(8):
            m = {k: base[k] for k in innames if k != "x_in"}
            m["x_in"] = x_cur[c]
            in_maps.append(m)
        res = run_bass_kernel_spmd(nc, in_maps, core_ids=list(range(8)))
        x_cur = [np.asarray(res.results[c]["y_out"]) for c in range(8)]
    return np.stack([x_cur[c].T for c in range(8)], axis=0).astype(np.float32)


# version 07
# speedup vs baseline: 1.0933x; 1.0933x over previous
import os
os.environ.setdefault("NEURON_SCRATCHPAD_PAGE_SIZE", "768")
import math
import numpy as np
from contextlib import ExitStack

import concourse.bass as bass
import concourse.mybir as mybir
from concourse.bass_utils import run_bass_kernel_spmd

F32 = mybir.dt.float32
BF16 = mybir.dt.bfloat16
AF = mybir.ActivationFunctionType
ALU = mybir.AluOpType
AX = mybir.AxisListType

T = 2048
D = 2048
NCH = 16
TQ = 512
NTQ = 4
H0 = 8192
NH0 = 64
H1 = 5632
NH1 = 44
KV = 512
LQG = 4096

RS2048 = 1.0 / math.sqrt(2048.0)
RS8192 = 1.0 / math.sqrt(8192.0)
RS128 = 1.0 / math.sqrt(128.0)

SUBLAYERS = ("ffn0", "att", "ffn0", "ffn1", "scan", "ffn1")
UNFUSED = False



def host_consts():
    H = np.array([[1.0]], dtype=np.float32)
    for _ in range(7):
        H = np.block([[H, H], [H, -H]]).astype(np.float32)
    i = np.arange(1, 65, dtype=np.float32)
    thetas = np.float32(10000.0) ** (-2.0 * (i - 1.0) / np.float32(128.0))
    ang = np.arange(1, T + 1, dtype=np.float32)[:, None] * thetas
    cosx = np.concatenate([np.cos(ang), np.cos(ang)], axis=1).astype(np.float32).T
    sinx = np.concatenate([np.sin(ang), np.sin(ang)], axis=1).astype(np.float32).T
    maskc = np.where(np.arange(128)[:, None] >= np.arange(128)[None, :],
                     0.0, -1e30).astype(np.float32)
    return {
        "ident": np.ascontiguousarray(np.eye(128, dtype=np.float32)),
        "h128": np.ascontiguousarray(H),
        "cosx": np.ascontiguousarray(cosx),
        "sinx": np.ascontiguousarray(sinx),
        "maskc": np.ascontiguousarray(maskc),
        "ones_a": np.ones((128, 1), dtype=np.float32),
        "ones_b": np.ones((1, 128), dtype=np.float32),
    }


def vec_chunks(v):
    v = np.asarray(v, dtype=np.float32)
    n = v.shape[0] // 128
    return np.ascontiguousarray(v.reshape(n, 128).T)




class K:
    def __init__(self, nc, es):
        self.nc = nc
        self.sem = {}
        self.c = {}
        for s in ("pe", "act", "dve", "pool", "sd", "gd", "ad"):
            self.sem[s] = es.enter_context(nc.semaphore("sem_" + s))
            self.c[s] = 0

    def inc(self, ins, s, n=1):
        ins.then_inc(self.sem[s], n)
        self.c[s] += n
        return self.c[s]

    def w(self, eng, s, v):
        if v > 0:
            eng.wait_ge(self.sem[s], v)

    def fence(self):
        with self.nc.Block() as blk:
            gd, sd, ad = self.c["gd"], self.c["sd"], self.c["ad"]

            @blk.gpsimd
            def _(e):
                self.w(e, "gd", gd)

            @blk.sync
            def _(e):
                self.w(e, "sd", sd)

            @blk.scalar
            def _(e):
                self.w(e, "ad", ad)


_SBUF_SERIAL = [0]


def sbuf(K_, es, t, name, shape, dt):
    _SBUF_SERIAL[0] += 1
    t[name] = es.enter_context(
        K_.nc.sbuf_tensor(f"{name}_{_SBUF_SERIAL[0]}", shape, dt))




def p0_emit(K_, t, es_p0, need):
    nc = K_.nc
    with nc.Block() as blk:
        @blk.gpsimd
        def _(g):
            casts = []
            if "ffn0" in need:
                casts += [("wbf_up", "wup"), ("wbf_gate", "wgate")]
            for a, b in casts:
                K_.inc(g.dma_start(t[a][:, :], t[b]), "gd", 16)
            K_.inc(g.dma_start(t["sb_identbf"][:], t["ident"]), "gd", 16)

        @blk.sync
        def _(s):
            for name in ("bqv", "bkv", "bvv", "bov", "suug", "sudp", "wscu",
                         "wscg", "wscdp", "h128", "maskc", "ones_a", "ones_b"):
                K_.inc(s.dma_start(t["sb_" + name][:], t[name]), "sd", 16)

    K_.fence()

    with nc.Block() as blk:
        @blk.gpsimd
        def _(g):
            casts = []
            if "att" in need:
                casts += [("wbf_q", "wq"), ("wbf_k", "wk"), ("wbf_v", "wv"),
                          ("wbf_o", "wo")]
            if "scan" in need:
                casts += [("wbf_l1", "wl1")]
            for a, b in casts:
                K_.inc(g.dma_start(t[a][:, :], t[b]), "gd", 16)
            if "att" in need:
                K_.inc(g.dma_start(t["sb_cos"][:], t["cosx"]), "gd", 16)
                K_.inc(g.dma_start(t["sb_sin"][:], t["sinx"]), "gd", 16)

    with nc.Block() as blk:
        @blk.vector
        def _(v):
            for i_ in range(6):
                v.memset(t["ps%d" % i_][:], 0.0)
            v.memset(t["sb_eps"][:], 1e-5)
            v.tensor_scalar_mul(t["sb_wscu_su"][:], t["sb_wscu"][:], RS2048)
            v.tensor_mul(t["sb_wscu_su"][:], t["sb_wscu_su"][:], t["sb_sudp"][:])
            v.tensor_scalar_mul(t["sb_wscg_s"][:], t["sb_wscg"][:], RS2048)
            v.tensor_scalar_mul(t["sb_wscdp_h"][:], t["sb_wscdp"][:], 0.5)
            v.tensor_scalar_mul(t["sb_bq_s"][:], t["sb_bqv"][:], RS128)
            for c in range(NCH):
                v.tensor_scalar(t["sb_h128su"][:, c * 128:(c + 1) * 128],
                                t["sb_h128"][:], t["sb_suug"][:, c:c + 1], None,
                                ALU.mult)
            v.tensor_copy(t["sb_h128bf"][:], t["sb_h128"][:])

    if "ffn0" in need:
        with ExitStack() as es:
            sbuf(K_, es, t, "p0_bufA0", [128, H0], F32)
            sbuf(K_, es, t, "p0_bufA1", [128, H0], F32)
            sbuf(K_, es, t, "p0_bufB", [128, H0], F32)
            sbuf(K_, es, t, "p0_bufC0", [128, H0], BF16)
            sbuf(K_, es, t, "p0_bufC1", [128, H0], BF16)
            ldw = {}
            stw = {}
            for ch in range(NCH):
                with nc.Block() as blk:
                    bufA = t["p0_bufA%d" % (ch % 2)]
                    bufC = t["p0_bufC%d" % (ch % 2)]

                    @blk.gpsimd
                    def _(g):
                        if ch == 0:
                            ldw[0] = K_.inc(g.dma_start(
                                t["p0_bufA0"][:], t["wdown"][0:128, :]),
                                "gd", 16)
                        if ch + 1 < NCH:
                            ldw[ch + 1] = K_.inc(g.dma_start(
                                t["p0_bufA%d" % ((ch + 1) % 2)][:],
                                t["wdown"][(ch + 1) * 128:(ch + 2) * 128, :]),
                                "gd", 16)

                    @blk.vector
                    def _(v):
                        v.wait_ge(K_.sem["gd"], ldw[ch])
                        src, dst = bufA, t["p0_bufB"]
                        for s_ in range(6):
                            h = 128 << s_
                            av = src[:].rearrange("p (nb two h) -> p nb two h",
                                                  two=2, h=h)
                            bv = dst[:].rearrange("p (nb two h) -> p nb two h",
                                                  two=2, h=h)
                            v.tensor_add(bv[:, :, 0, :], av[:, :, 0, :],
                                         av[:, :, 1, :])
                            v.tensor_sub(bv[:, :, 1, :], av[:, :, 0, :],
                                         av[:, :, 1, :])
                            src, dst = dst, src
                        if ch >= 2:
                            v.wait_ge(K_.sem["sd"], stw[ch - 2])
                        K_.inc(v.tensor_copy(bufC[:], src[:]), "dve", 1)

                    @blk.sync
                    def _(sy):
                        sy.wait_ge(K_.sem["dve"], K_.c["dve"])
                        stw[ch] = K_.inc(sy.dma_start(
                            t["wbf_down"][ch * 128:(ch + 1) * 128, :],
                            bufC[:]), "sd", 16)
            K_.fence()

    if "ffn1" in need:
        with ExitStack() as es:
            sbuf(K_, es, t, "p0_qb", [128, 2 * H1], F32)
            sbuf(K_, es, t, "p0_scrap", [128, 2 * H1], F32)
            sbuf(K_, es, t, "p0_parts", [128, 64], F32)
            sbuf(K_, es, t, "p0_g1", [128, H1], F32)
            sbuf(K_, es, t, "p0_obuf", [128, 2 * H1], BF16)
            sbuf(K_, es, t, "p0_scbc", [128, 96], F32)
            QSPEC = (("w1", "wbf_w1", NH1, 2048), ("w2", "wbf_w2", NH1, 2048),
                     ("w3", "wbf_w3", NCH, H1))
            for qi, (wname, dname, nch, ncols) in enumerate(QSPEC):
                _p0_quant_gamma(K_, t, wname, qi, nch, ncols)
            K_.fence()
            for qi, (wname, dname, nch, ncols) in enumerate(QSPEC):
                _p0_quant_apply(K_, t, wname, dname, qi, nch, ncols)
            K_.fence()

    K_.fence()


def _p0_quant_gamma(K_, t, wname, qi, nch, ncols):
    nc = K_.nc
    qb, scrap, parts = t["p0_qb"], t["p0_scrap"], t["p0_parts"]
    nelem = float(nch * 128 * ncols)

    with nc.Block() as blk:
        ld_ev = []
        b_act = K_.c["act"]
        b_dve_q = K_.c["dve"]

        @blk.gpsimd
        def _(g):
            for ch in range(nch):
                if ch >= 2:
                    g.wait_ge(K_.sem["act"], b_act + ch - 1)
                ld_ev.append(K_.inc(
                    g.dma_start(qb[:, (ch % 2) * ncols:(ch % 2 + 1) * ncols],
                                t[wname][ch * 128:(ch + 1) * 128, :]), "gd", 16))

        @blk.scalar
        def _(s):
            for ch in range(nch):
                s.wait_ge(K_.sem["gd"], ld_ev[ch])
                if ch >= 2:
                    s.wait_ge(K_.sem["dve"], b_dve_q + ch - 1)
                K_.inc(s.activation(scrap[:, (ch % 2) * ncols:(ch % 2) * ncols + ncols],
                                    qb[:, (ch % 2) * ncols:(ch % 2 + 1) * ncols],
                                    AF.Abs), "act", 1)

        @blk.vector
        def _(v):
            for ch in range(nch):
                v.wait_ge(K_.sem["act"], b_act + ch + 1)
                K_.inc(v.reduce_sum(t["ps3"][:, ch:ch + 1],
                                    scrap[:, (ch % 2) * ncols:(ch % 2) * ncols + ncols],
                                    AX.X), "dve", 1)

    with nc.Block() as blk:
        @blk.vector
        def _(v):
            K_.inc(v.reduce_sum(t["ps2"][:, 8:9], t["ps3"][:, 0:nch], AX.X),
                   "dve", 1)

        @blk.scalar
        def _(s):
            s.wait_ge(K_.sem["dve"], K_.c["dve"])
            K_.inc(s.copy(scrap[:, 0:512], t["ps2"][:, 0:512]), "act", 1)

    with nc.Block() as blk:
        @blk.tensor
        def _(tn):
            tn.wait_ge(K_.sem["dve"], K_.c["dve"])
            K_.inc(tn.matmul(t["ps0"][0:1, 0:512], t["sb_ones_a"][:],
                             scrap[:, 0:512], start=True, stop=True), "pe", 1)

        @blk.vector
        def _(v):
            v.wait_ge(K_.sem["pe"], K_.c["pe"])
            v.tensor_scalar_mul(scrap[0:1, 1024:1536], t["ps0"][0:1, 0:512],
                                1.0 / nelem)
            K_.inc(v.tensor_scalar_add(scrap[0:1, 1024:1536],
                                       scrap[0:1, 1024:1536], 1e-8), "dve", 1)

        @blk.scalar
        def _(s):
            s.wait_ge(K_.sem["dve"], K_.c["dve"])
            s.activation(scrap[0:1, 1536:2048], scrap[0:1, 1024:1536], AF.Abs)
            s.activation(scrap[0:1, 1024:1536], scrap[0:1, 1536:2048], AF.Ln)
            K_.inc(s.activation(scrap[0:1, 1536:2048], scrap[0:1, 1024:1536],
                                AF.Exp, scale=-1.0), "act", 1)

        @blk.vector
        def _(v):
            v.wait_ge(K_.sem["act"], K_.c["act"])
            K_.inc(v.tensor_copy(scrap[0:1, 2560:3072],
                                 scrap[0:1, 1536:2048]), "dve", 1)

        @blk.tensor
        def _(tn):
            tn.wait_ge(K_.sem["dve"], K_.c["dve"])
            K_.inc(tn.matmul(t["ps1"][:, 0:512], t["sb_ones_b"][:],
                             scrap[0:1, 2560:3072], start=True, stop=True),
                   "pe", 1)

        @blk.scalar
        def _(s):
            s.wait_ge(K_.sem["pe"], K_.c["pe"])
            K_.inc(s.copy(t["p0_scbc"][:, qi * 32:(qi + 1) * 32],
                          t["ps1"][:, 8:40]), "act", 1)

        if "qdbg" in t:
            @blk.vector
            def _(v):
                v.wait_ge(K_.sem["act"], K_.c["act"])
                v.tensor_copy(scrap[0:1, 4096 + 0:4096 + 16], t["ps0"][0:1, 0:16])
                v.tensor_copy(scrap[0:1, 4096 + 16:4096 + 32],
                              scrap[0:1, 1024 + 0:1024 + 16])
                v.tensor_copy(scrap[0:1, 4096 + 32:4096 + 48],
                              scrap[0:1, 2560 + 0:2560 + 16])
                K_.inc(v.tensor_copy(scrap[0:1, 4096 + 48:4096 + 64],
                                     t["ps1"][0:1, 0:16]), "dve", 1)

            @blk.gpsimd
            def _(g):
                g.wait_ge(K_.sem["dve"], K_.c["dve"])
                K_.inc(g.dma_start(t["qdbg"][:, qi * 256:qi * 256 + 96],
                                   t["p0_scbc"][:, 0:96]), "gd", 16)
                K_.inc(g.dma_start(t["qdbg"][0:2, qi * 256 + 96:qi * 256 + 96 + 128],
                                   scrap[0:2, 4096:4096 + 128]), "gd", 16)


def _p0_quant_apply(K_, t, wname, dname, qi, nch, ncols):
    nc = K_.nc
    qb, scrap = t["p0_qb"], t["p0_scrap"]
    with nc.Block() as blk:
        gd0 = K_.c["gd"]
        b_dve = K_.c["dve"]

        @blk.gpsimd
        def _(g):
            for ch in range(nch):
                if ch >= 2:
                    g.wait_ge(K_.sem["dve"], b_dve + ch - 1)
                K_.inc(g.dma_start(qb[:, (ch % 2) * ncols:(ch % 2 + 1) * ncols],
                                   t[wname][ch * 128:(ch + 1) * 128, :]),
                       "gd", 16)

        sd0 = K_.c["sd"]

        @blk.vector
        def _(v):
            for ch in range(nch):
                sl = qb[:, (ch % 2) * ncols:(ch % 2 + 1) * ncols]
                osl = t["p0_obuf"][:, (ch % 2) * ncols:(ch % 2 + 1) * ncols]
                v.wait_ge(K_.sem["gd"], gd0 + 16 * (ch + 1))
                if ch >= 2:
                    v.wait_ge(K_.sem["sd"], sd0 + 16 * (ch - 1))
                v.tensor_scalar(sl, sl, t["p0_scbc"][:, qi * 32:qi * 32 + 1],
                                None, ALU.mult)
                v.tensor_single_scalar(scrap[:, 0:ncols], sl, 0.5, ALU.is_gt)
                v.tensor_single_scalar(t["p0_g1"][:, 0:ncols], sl, -0.5, ALU.is_lt)
                K_.inc(v.tensor_sub(osl, scrap[:, 0:ncols], t["p0_g1"][:, 0:ncols]),
                       "dve", 1)

        @blk.sync
        def _(sy):
            for ch in range(nch):
                sy.wait_ge(K_.sem["dve"], b_dve + ch + 1)
                K_.inc(sy.dma_start(
                    t[dname][ch * 128:(ch + 1) * 128, :],
                    t["p0_obuf"][:, (ch % 2) * ncols:(ch % 2 + 1) * ncols]),
                    "sd", 16)




def emit_ln_tq(K_, blk, t, x_dram, tq, dst, cs, off):
    ld_ev = []
    b_pe = K_.c["pe"]
    b_act = K_.c["act"]
    xq, sq, st = t["ln_xq"], t["ln_sq"], t["ln_st"]

    @blk.gpsimd
    def _(g):
        for ch in range(NCH):
            ld_ev.append(K_.inc(
                g.dma_start(xq[:, ch * TQ:(ch + 1) * TQ],
                            x_dram[ch * 128:(ch + 1) * 128,
                                   tq * TQ:(tq + 1) * TQ]), "gd", 16))

    @blk.scalar
    def _(s):
        for ch in range(NCH):
            s.wait_ge(K_.sem["gd"], ld_ev[ch])
            if ch >= 2:
                s.wait_ge(K_.sem["pe"], b_pe + ch - 1)
            K_.inc(s.activation(sq[:, (ch % 2) * TQ:(ch % 2 + 1) * TQ],
                                xq[:, ch * TQ:(ch + 1) * TQ], AF.Square),
                   "act", 1)

    @blk.tensor
    def _(tn):
        for ch in range(NCH):
            tn.wait_ge(K_.sem["gd"], ld_ev[ch])
            tn.wait_ge(K_.sem["act"], b_act + ch + 1)
            tn.matmul(t["ps0"][0:1, :], t["sb_ones_a"][:],
                      xq[:, ch * TQ:(ch + 1) * TQ],
                      start=(ch == 0), stop=(ch == NCH - 1))
            K_.inc(tn.matmul(t["ps1"][0:1, :], t["sb_ones_a"][:],
                             sq[:, (ch % 2) * TQ:(ch % 2 + 1) * TQ],
                             start=(ch == 0), stop=(ch == NCH - 1)), "pe", 1)

    @blk.vector
    def _(v):
        v.wait_ge(K_.sem["pe"], K_.c["pe"])
        m, msq, var = st[0:1, 0:TQ], st[0:1, TQ:2 * TQ], st[0:1, 2 * TQ:3 * TQ]
        v.tensor_scalar_mul(m, t["ps0"][0:1, :], 1.0 / D)
        v.tensor_scalar_mul(msq, t["ps1"][0:1, :], 1.0 / D)
        v.tensor_mul(var, m, m)
        K_.inc(v.tensor_sub(var, msq, var), "dve", 1)

    @blk.scalar
    def _(s):
        s.wait_ge(K_.sem["dve"], K_.c["dve"])
        s.activation(st[0:1, 3 * TQ:4 * TQ], st[0:1, 2 * TQ:3 * TQ],
                     AF.Ln, bias=t["sb_eps"][0:1, 0:1])
        K_.inc(s.activation(st[0:1, 4 * TQ:5 * TQ], st[0:1, 3 * TQ:4 * TQ],
                            AF.Exp, scale=-0.5), "act", 1)

    @blk.vector
    def _(v):
        v.wait_ge(K_.sem["act"], K_.c["act"])
        rstd, nmr = st[0:1, 4 * TQ:5 * TQ], st[0:1, 5 * TQ:6 * TQ]
        v.tensor_mul(nmr, st[0:1, 0:TQ], rstd)
        K_.inc(v.tensor_scalar_mul(nmr, nmr, -1.0), "dve", 1)

    @blk.tensor
    def _(tn):
        tn.wait_ge(K_.sem["dve"], K_.c["dve"])
        tn.matmul(t["ps2"][:], t["sb_ones_b"][:], st[0:1, 4 * TQ:5 * TQ],
                  start=True, stop=True)
        K_.inc(tn.matmul(t["ps3"][:], t["sb_ones_b"][:], st[0:1, 5 * TQ:6 * TQ],
                         start=True, stop=True), "pe", 1)

    ln_ev = []

    @blk.vector
    def _(v):
        v.wait_ge(K_.sem["pe"], K_.c["pe"])
        for ch in range(NCH):
            v.tensor_mul(t["ln_tmp"][:], xq[:, ch * TQ:(ch + 1) * TQ], t["ps2"][:])
            ln_ev.append(K_.inc(
                v.tensor_add(dst[:, ch * cs + off:ch * cs + off + TQ],
                             t["ln_tmp"][:], t["ps3"][:]), "dve", 1))

    return ln_ev


def alloc_ln_ws(K_, es, t):
    sbuf(K_, es, t, "ln_xq", [128, 16 * TQ], F32)
    sbuf(K_, es, t, "ln_sq", [128, 2 * TQ], F32)
    sbuf(K_, es, t, "ln_st", [1, 8 * TQ], F32)
    sbuf(K_, es, t, "ln_tmp", [128, TQ], F32)




def emit_ffn0(K_, t, x_in, x_out):
    nc = K_.nc
    for tq in range(NTQ):
        with ExitStack() as es_tq:
            sbuf(K_, es_tq, t, "f0_a", [128, 16 * TQ], BF16)
            sbuf(K_, es_tq, t, "f0_c", [128, NH0 * TQ], BF16)
            K_.fence()
            with ExitStack() as es_a:
                alloc_ln_ws(K_, es_a, t)
                sbuf(K_, es_a, t, "f0_bt", [128, TQ], BF16)
                with nc.Block() as blk:
                    ln_ev = emit_ln_tq(K_, blk, t, x_in, tq, t["f0_a"], TQ, 0)
                    b_pe = K_.c["pe"]
                    b_act = K_.c["act"]

                    @blk.tensor
                    def _(tn):
                        for ch in range(NCH):
                            tn.wait_ge(K_.sem["dve"], ln_ev[ch])
                            if ch >= 2:
                                tn.wait_ge(K_.sem["act"], b_act + ch - 1)
                            K_.inc(tn.matmul(
                                t["ps4" if ch % 2 == 0 else "ps5"][:],
                                t["sb_h128su"][:, ch * 128:(ch + 1) * 128],
                                t["f0_a"][:, ch * TQ:(ch + 1) * TQ],
                                start=True, stop=True), "pe", 1)

                    @blk.scalar
                    def _(s):
                        for ch in range(NCH):
                            s.wait_ge(K_.sem["pe"], b_pe + ch + 1)
                            K_.inc(s.copy(t["f0_a"][:, ch * TQ:(ch + 1) * TQ],
                                          t["ps4" if ch % 2 == 0 else "ps5"][:]),
                                   "act", 1)

                    @blk.vector
                    def _(v):
                        v.wait_ge(K_.sem["act"], K_.c["act"])
                        a, bt = t["f0_a"], t["f0_bt"]
                        for s_ in range(4):
                            hc = 1 << s_
                            for b in range(NCH // (2 * hc)):
                                for j in range(hc):
                                    lo = (b * 2 * hc + j) * TQ
                                    hi = lo + hc * TQ
                                    v.tensor_sub(bt[:], a[:, lo:lo + TQ],
                                                 a[:, hi:hi + TQ])
                                    v.tensor_add(a[:, lo:lo + TQ],
                                                 a[:, lo:lo + TQ],
                                                 a[:, hi:hi + TQ])
                                    v.tensor_copy(a[:, hi:hi + TQ], bt[:])

            K_.fence()
            with ExitStack() as es_b:
                sbuf(K_, es_b, t, "f0_wu0", [128, 16 * TQ], BF16)
                sbuf(K_, es_b, t, "f0_wu1", [128, 16 * TQ], BF16)
                sbuf(K_, es_b, t, "f0_wg0", [128, 16 * TQ], BF16)
                sbuf(K_, es_b, t, "f0_wg1", [128, 16 * TQ], BF16)
                sbuf(K_, es_b, t, "f0_sil0", [128, TQ], F32)
                sbuf(K_, es_b, t, "f0_sil1", [128, TQ], F32)
                sbuf(K_, es_b, t, "f0_cp0", [128, TQ], BF16)
                sbuf(K_, es_b, t, "f0_cp1", [128, TQ], BF16)
                ldu = {}
                ldg = {}
                NMS = NH0 // 4
                with nc.Block() as blk:
                    b_pe = K_.c["pe"]
                    b_act = K_.c["act"]
                    b_dve = K_.c["dve"]

                    @blk.sync
                    def _(sy):
                        for ms in range(NMS):
                            if ms >= 2:
                                sy.wait_ge(K_.sem["pe"],
                                           b_pe + 2 * (4 * (ms - 1)) + 1)
                            wu = t["f0_wu0"] if ms % 2 == 0 else t["f0_wu1"]
                            wg = t["f0_wg0"] if ms % 2 == 0 else t["f0_wg1"]
                            for kc in range(NCH):
                                K_.inc(sy.dma_start_transpose(
                                    wu[:, kc * TQ:(kc + 1) * TQ],
                                    t["wbf_up"][ms * 512:(ms + 1) * 512,
                                                kc * 128:(kc + 1) * 128]),
                                    "sd", 16)
                                ldu[ms] = K_.inc(sy.dma_start_transpose(
                                    wg[:, kc * TQ:(kc + 1) * TQ],
                                    t["wbf_gate"][ms * 512:(ms + 1) * 512,
                                                  kc * 128:(kc + 1) * 128]),
                                    "sd", 16)

                    @blk.tensor
                    def _(tn):
                        for mc in range(NH0):
                            ms, mj = mc // 4, mc % 4
                            wu = t["f0_wu0"] if ms % 2 == 0 else t["f0_wu1"]
                            wg = t["f0_wg0"] if ms % 2 == 0 else t["f0_wg1"]
                            psU = t["ps0"] if mc % 2 == 0 else t["ps1"]
                            psG = t["ps2"] if mc % 2 == 0 else t["ps3"]
                            psH = t["ps4"] if mc % 2 == 0 else t["ps5"]
                            tn.wait_ge(K_.sem["sd"], ldu[ms])
                            if mc >= 2:
                                tn.wait_ge(K_.sem["dve"], b_dve + mc - 1)
                                tn.wait_ge(K_.sem["act"], b_act + 2 * mc - 3)
                            for kc in range(NCH):
                                tn.matmul(psU[:],
                                          wu[:, kc * TQ + mj * 128:
                                             kc * TQ + (mj + 1) * 128],
                                          t["f0_a"][:, kc * TQ:(kc + 1) * TQ],
                                          start=(kc == 0), stop=(kc == NCH - 1))
                            for kc in range(NCH):
                                ins = tn.matmul(psG[:],
                                                wg[:, kc * TQ + mj * 128:
                                                   kc * TQ + (mj + 1) * 128],
                                                t["f0_a"][:, kc * TQ:(kc + 1) * TQ],
                                                start=(kc == 0),
                                                stop=(kc == NCH - 1))
                            K_.inc(ins, "pe", 1)
                            tn.wait_ge(K_.sem["dve"], b_dve + mc + 1)
                            if mc >= 2:
                                tn.wait_ge(K_.sem["act"], b_act + 2 * mc - 2)
                            cpre = t["f0_cp0"] if mc % 2 == 0 else t["f0_cp1"]
                            K_.inc(tn.matmul(psH[:], t["sb_h128bf"][:], cpre[:],
                                             start=True, stop=True), "pe", 1)

                    @blk.scalar
                    def _(s):
                        for mc in range(NH0):
                            psG = t["ps2"] if mc % 2 == 0 else t["ps3"]
                            psH = t["ps4"] if mc % 2 == 0 else t["ps5"]
                            sil = t["f0_sil0"] if mc % 2 == 0 else t["f0_sil1"]
                            s.wait_ge(K_.sem["pe"], b_pe + 2 * mc + 1)
                            if mc >= 2:
                                s.wait_ge(K_.sem["dve"], b_dve + mc - 1)
                            K_.inc(s.activation(sil[:], psG[:], AF.Silu,
                                                scale=t["sb_wscg_s"][:, mc:mc + 1]),
                                   "act", 1)
                            s.wait_ge(K_.sem["pe"], b_pe + 2 * mc + 2)
                            K_.inc(s.activation(t["f0_c"][:, mc * TQ:(mc + 1) * TQ],
                                                psH[:], AF.Copy, scale=RS8192),
                                   "act", 1)

                    @blk.vector
                    def _(v):
                        for mc in range(NH0):
                            psU = t["ps0"] if mc % 2 == 0 else t["ps1"]
                            sil = t["f0_sil0"] if mc % 2 == 0 else t["f0_sil1"]
                            cpre = t["f0_cp0"] if mc % 2 == 0 else t["f0_cp1"]
                            v.wait_ge(K_.sem["act"], b_act + 2 * mc + 1)
                            if mc >= 2:
                                v.wait_ge(K_.sem["pe"], b_pe + 2 * mc - 2)
                            K_.inc(v.scalar_tensor_tensor(
                                cpre[:], sil[:], t["sb_wscu_su"][:, mc:mc + 1],
                                psU[:], ALU.mult, ALU.mult), "dve", 1)

            K_.fence()
            with ExitStack() as es_c:
                sbuf(K_, es_c, t, "f0_wd", [128, 8 * TQ], BF16)
                sbuf(K_, es_c, t, "f0_xold", [128, 16 * TQ], F32)
                sbuf(K_, es_c, t, "f0_xnew", [128, 16 * TQ], F32)
                _emit_down(K_, t, x_in, x_out, tq, "f0_c", NH0, "f0_wd",
                           "wbf_down", "f0_xold", "f0_xnew",
                           scale_ap="sb_wscdp_h")


def _emit_down(K_, t, x_in, x_out, tq, cname, nkh, wdname, wbfname,
               xoldname, xnewname, scale_ap=None, c_stride=TQ, c_off=0):
    nc = K_.nc
    with nc.Block() as blk:
        b_pe = K_.c["pe"]
        b_dve = K_.c["dve"]
        ld = {}
        xld = []
        st_ev = []

        @blk.gpsimd
        def _(g):
            for mc in range(NCH):
                xld.append(K_.inc(g.dma_start(
                    t[xoldname][:, mc * TQ:(mc + 1) * TQ],
                    x_in[mc * 128:(mc + 1) * 128, tq * TQ:(tq + 1) * TQ]),
                    "gd", 16))

        @blk.sync
        def _(sy):
            for mq in range(4):
                for kh in range(nkh):
                    gi = mq * nkh + kh
                    if gi >= 8:
                        sy.wait_ge(K_.sem["pe"], b_pe + gi - 7)
                    ld[gi] = K_.inc(sy.dma_start_transpose(
                        t[wdname][:, (gi % 8) * TQ:(gi % 8 + 1) * TQ],
                        t[wbfname][mq * 512:(mq + 1) * 512,
                                   kh * 128:(kh + 1) * 128]), "sd", 16)

        @blk.tensor
        def _(tn):
            for mq in range(4):
                if mq >= 1:
                    tn.wait_ge(K_.sem["dve"], b_dve + 4 * mq)
                for kh in range(nkh):
                    gi = mq * nkh + kh
                    tn.wait_ge(K_.sem["sd"], ld[gi])
                    for mj in range(4):
                        ins = tn.matmul(
                            t["ps%d" % mj][:],
                            t[wdname][:, (gi % 8) * TQ + mj * 128:
                                      (gi % 8) * TQ + (mj + 1) * 128],
                            t[cname][:, kh * c_stride + c_off:
                                     kh * c_stride + c_off + TQ],
                            start=(kh == 0), stop=(kh == nkh - 1))
                    K_.inc(ins, "pe", 1)

        @blk.vector
        def _(v):
            for mq in range(4):
                v.wait_ge(K_.sem["pe"], b_pe + (mq + 1) * nkh)
                for mj in range(4):
                    mc = mq * 4 + mj
                    v.wait_ge(K_.sem["gd"], xld[mc])
                    sc = (t[scale_ap][:, mc:mc + 1] if scale_ap else 0.5)
                    st_ev.append(K_.inc(v.scalar_tensor_tensor(
                        t[xnewname][:, mc * TQ:(mc + 1) * TQ],
                        t["ps%d" % mj][:], sc,
                        t[xoldname][:, mc * TQ:(mc + 1) * TQ],
                        ALU.mult, ALU.add), "dve", 1))

        @blk.gpsimd
        def _(g):
            for mc in range(NCH):
                g.wait_ge(K_.sem["dve"], st_ev[mc])
                K_.inc(g.dma_start(
                    x_out[mc * 128:(mc + 1) * 128, tq * TQ:(tq + 1) * TQ],
                    t[xnewname][:, mc * TQ:(mc + 1) * TQ]), "gd", 16)




def emit_ffn1(K_, t, x_in, x_out):
    nc = K_.nc
    NMS = NH1 // 4
    TP = 1024
    for tp in range(2):
        with ExitStack() as es_tq:
            sbuf(K_, es_tq, t, "f1_y", [128, 16 * TP], BF16)
            sbuf(K_, es_tq, t, "f1_c", [128, NH1 * TP], BF16)
            K_.fence()
            with ExitStack() as es_a:
                alloc_ln_ws(K_, es_a, t)
                with nc.Block() as blk:
                    emit_ln_tq(K_, blk, t, x_in, 2 * tp, t["f1_y"], TP, 0)
                with nc.Block() as blk:
                    emit_ln_tq(K_, blk, t, x_in, 2 * tp + 1, t["f1_y"], TP, TQ)

            K_.fence()
            with ExitStack() as es_b:
                sbuf(K_, es_b, t, "f1_wA", [128, 16 * TP], BF16)
                sbuf(K_, es_b, t, "f1_wB", [128, 16 * TP], BF16)
                sbuf(K_, es_b, t, "f1_sig0", [128, TP], F32)
                sbuf(K_, es_b, t, "f1_sig1", [128, TP], F32)
                ld1 = {}
                with nc.Block() as blk:
                    b_pe = K_.c["pe"]
                    b_act = K_.c["act"]
                    b_dve = K_.c["dve"]

                    @blk.sync
                    def _(sy):
                        for ms in range(NMS):
                            if ms >= 2:
                                sy.wait_ge(K_.sem["pe"], b_pe + 4 * (ms - 1) + 1)
                            wb = t["f1_wA"] if ms % 2 == 0 else t["f1_wB"]
                            for kc in range(NCH):
                                K_.inc(sy.dma_start_transpose(
                                    wb[:, kc * TP:kc * TP + 512],
                                    t["wbf_w1"][ms * 512:(ms + 1) * 512,
                                                kc * 128:(kc + 1) * 128]),
                                    "sd", 16)
                                ld1[ms] = K_.inc(sy.dma_start_transpose(
                                    wb[:, kc * TP + 512:kc * TP + 1024],
                                    t["wbf_w2"][ms * 512:(ms + 1) * 512,
                                                kc * 128:(kc + 1) * 128]),
                                    "sd", 16)

                    @blk.tensor
                    def _(tn):
                        for mc in range(NH1):
                            ms, mj = mc // 4, mc % 4
                            wb = t["f1_wA"] if ms % 2 == 0 else t["f1_wB"]
                            tn.wait_ge(K_.sem["sd"], ld1[ms])
                            if mc >= 1:
                                tn.wait_ge(K_.sem["dve"], b_dve + mc)
                                tn.wait_ge(K_.sem["act"], b_act + mc)
                            for kc in range(NCH):
                                for tt in range(2):
                                    tn.matmul(t["ps%d" % tt][:],
                                              wb[:, kc * TP + mj * 128:
                                                 kc * TP + (mj + 1) * 128],
                                              t["f1_y"][:, kc * TP + tt * TQ:
                                                        kc * TP + (tt + 1) * TQ],
                                              start=(kc == 0),
                                              stop=(kc == NCH - 1))
                            for kc in range(NCH):
                                for tt in range(2):
                                    ins = tn.matmul(
                                        t["ps%d" % (2 + tt)][:],
                                        wb[:, kc * TP + 512 + mj * 128:
                                           kc * TP + 512 + (mj + 1) * 128],
                                        t["f1_y"][:, kc * TP + tt * TQ:
                                                  kc * TP + (tt + 1) * TQ],
                                        start=(kc == 0), stop=(kc == NCH - 1))
                            K_.inc(ins, "pe", 1)

                    @blk.scalar
                    def _(s):
                        for mc in range(NH1):
                            sig = t["f1_sig0"] if mc % 2 == 0 else t["f1_sig1"]
                            s.wait_ge(K_.sem["pe"], b_pe + mc + 1)
                            if mc >= 2:
                                s.wait_ge(K_.sem["dve"], b_dve + mc - 1)
                            s.activation(sig[:, 0:TQ], t["ps0"][:], AF.Sigmoid)
                            K_.inc(s.activation(sig[:, TQ:TP], t["ps1"][:],
                                                AF.Sigmoid), "act", 1)

                    @blk.vector
                    def _(v):
                        for mc in range(NH1):
                            sig = t["f1_sig0"] if mc % 2 == 0 else t["f1_sig1"]
                            v.wait_ge(K_.sem["act"], b_act + mc + 1)
                            v.tensor_mul(t["f1_c"][:, mc * TP:mc * TP + TQ],
                                         sig[:, 0:TQ], t["ps2"][:])
                            K_.inc(v.tensor_mul(
                                t["f1_c"][:, mc * TP + TQ:(mc + 1) * TP],
                                sig[:, TQ:TP], t["ps3"][:]), "dve", 1)

            K_.fence()
            with ExitStack() as es_c:
                sbuf(K_, es_c, t, "f1_wd", [128, 8 * TQ], BF16)
                sbuf(K_, es_c, t, "f1_xold", [128, 16 * TQ], F32)
                sbuf(K_, es_c, t, "f1_xnew", [128, 16 * TQ], F32)
                for tt in range(2):
                    _emit_down(K_, t, x_in, x_out, 2 * tp + tt, "f1_c", NH1,
                               "f1_wd", "wbf_w3", "f1_xold", "f1_xnew",
                               scale_ap=None, c_stride=TP, c_off=tt * TQ)
                    K_.fence()




def emit_scan(K_, t, x_in, x_out):
    nc = K_.nc
    K_.fence()
    with ExitStack() as es_s:
        sbuf(K_, es_s, t, "sc_lnb", [128, 16 * T], BF16)
        with ExitStack() as es_a:
            alloc_ln_ws(K_, es_a, t)
            for tq in range(NTQ):
                K_.fence()
                with nc.Block() as blk:
                    emit_ln_tq(K_, blk, t, x_in, tq, t["sc_lnb"], T, tq * TQ)

        K_.fence()
        with ExitStack() as es_b:
            sbuf(K_, es_b, t, "sc_w0", [128, 32 * 128], BF16)
            sbuf(K_, es_b, t, "sc_w1", [128, 32 * 128], BF16)
            sbuf(K_, es_b, t, "sc_q", [128, T], F32)
            sbuf(K_, es_b, t, "sc_g", [128, T], F32)
            sbuf(K_, es_b, t, "sc_h", [128, T], F32)
            sbuf(K_, es_b, t, "sc_xold", [128, T], F32)
            scld = {}
            sc_st = {}
            with nc.Block() as blk:
                @blk.sync
                def _(sy):
                    for pi, m in ((0, 0), (1, 16)):
                        for kc in range(NCH):
                            scld[0] = K_.inc(sy.dma_start_transpose(
                                t["sc_w0"][:, (pi * NCH + kc) * 128:
                                           (pi * NCH + kc + 1) * 128],
                                t["wbf_l1"][m * 128:(m + 1) * 128,
                                            kc * 128:(kc + 1) * 128]), "sd", 16)

            for p in range(NCH):
                with nc.Block() as blk:
                    wl = t["sc_w0"] if p % 2 == 0 else t["sc_w1"]
                    wln = t["sc_w1"] if p % 2 == 0 else t["sc_w0"]

                    @blk.sync
                    def _(sy):
                        if p + 1 < NCH:
                            for pi, m in ((0, p + 1), (1, p + 17)):
                                for kc in range(NCH):
                                    scld[p + 1] = K_.inc(sy.dma_start_transpose(
                                        wln[:, (pi * NCH + kc) * 128:
                                            (pi * NCH + kc + 1) * 128],
                                        t["wbf_l1"][m * 128:(m + 1) * 128,
                                                    kc * 128:(kc + 1) * 128]),
                                        "sd", 16)

                    xld = []

                    @blk.gpsimd
                    def _(g):
                        xld.append(K_.inc(
                            g.dma_start(t["sc_xold"][:],
                                        x_in[p * 128:(p + 1) * 128, :]),
                            "gd", 16))

                    @blk.tensor
                    def _(tn):
                        tn.wait_ge(K_.sem["sd"], scld[p])
                        for kc in range(NCH):
                            for tt in range(NTQ):
                                ins = tn.matmul(
                                    t["ps%d" % tt][:],
                                    wl[:, kc * 128:(kc + 1) * 128],
                                    t["sc_lnb"][:, kc * T + tt * TQ:
                                                kc * T + (tt + 1) * TQ],
                                    start=(kc == 0), stop=(kc == NCH - 1))
                        K_.inc(ins, "pe", 1)

                    @blk.vector
                    def _(v):
                        v.wait_ge(K_.sem["pe"], K_.c["pe"])
                        for tt in range(NTQ):
                            ins = v.tensor_single_scalar(
                                t["sc_q"][:, tt * TQ:(tt + 1) * TQ],
                                t["ps%d" % tt][:], 1e-20, ALU.max)
                        K_.inc(ins, "dve", 1)

                    @blk.tensor
                    def _(tn):
                        tn.wait_ge(K_.sem["dve"], K_.c["dve"])
                        for kc in range(NCH):
                            for tt in range(NTQ):
                                ins = tn.matmul(
                                    t["ps%d" % tt][:],
                                    wl[:, (NCH + kc) * 128:(NCH + kc + 1) * 128],
                                    t["sc_lnb"][:, kc * T + tt * TQ:
                                                kc * T + (tt + 1) * TQ],
                                    start=(kc == 0), stop=(kc == NCH - 1))
                        K_.inc(ins, "pe", 1)

                    @blk.scalar
                    def _(s):
                        s.wait_ge(K_.sem["pe"], K_.c["pe"])
                        for tt in range(NTQ):
                            ins = s.activation(t["sc_g"][:, tt * TQ:(tt + 1) * TQ],
                                               t["ps%d" % tt][:], AF.Sigmoid)
                        K_.inc(ins, "act", 1)

                    @blk.vector
                    def _(v):
                        v.wait_ge(K_.sem["act"], K_.c["act"])
                        v.tensor_single_scalar(t["sc_g"][:], t["sc_g"][:],
                                               1e-20, ALU.max)
                        if p >= 1:
                            v.wait_ge(K_.sem["gd"], sc_st[p - 1])
                        v.tensor_tensor_scan(t["sc_h"][:], t["sc_g"][:],
                                             t["sc_q"][:], 0.0, ALU.mult, ALU.add)
                        v.wait_ge(K_.sem["gd"], xld[0])
                        K_.inc(v.tensor_add(t["sc_h"][:], t["sc_h"][:],
                                            t["sc_xold"][:]), "dve", 1)

                    @blk.gpsimd
                    def _(g):
                        g.wait_ge(K_.sem["dve"], K_.c["dve"])
                        sc_st[p] = K_.inc(
                            g.dma_start(x_out[p * 128:(p + 1) * 128, :],
                                        t["sc_h"][:]), "gd", 16)




def _emit_rope(K_, v, t, src, dst):
    for h0 in (0, T // 2):
        tmp, tmp2 = t["at_rt1"], t["at_rt2"]
        sl = slice(h0, h0 + T // 2)
        v.tensor_mul(tmp[:], src[:, sl], t["sb_cos"][:, sl])
        v.tensor_mul(tmp2[0:64, :], src[64:128, sl], t["sb_sin"][64:128, sl])
        v.tensor_mul(tmp2[64:128, :], src[0:64, sl], t["sb_sin"][0:64, sl])
        v.tensor_sub(dst[0:64, sl], tmp[0:64, :], tmp2[0:64, :])
        ins = v.tensor_add(dst[64:128, sl], tmp[64:128, :], tmp2[64:128, :])
    return ins


def emit_att(K_, t, x_in, x_out):
    nc = K_.nc
    K_.fence()
    with ExitStack() as es_att:
        sbuf(K_, es_att, t, "at_kbf", [128, 4 * T], BF16)
        sbuf(K_, es_att, t, "at_vtok", [128, 4 * T], BF16)
        sbuf(K_, es_att, t, "at_qbf", [128, T], BF16)
        sbuf(K_, es_att, t, "at_qbf1", [128, T], BF16)
        sbuf(K_, es_att, t, "at_mx", [128, 64], F32)
        sbuf(K_, es_att, t, "at_sm", [128, 64], F32)
        with ExitStack() as es_1:
            sbuf(K_, es_1, t, "at_lnb", [128, 16 * T], BF16)
            with ExitStack() as es_a:
                alloc_ln_ws(K_, es_a, t)
                for tq in range(NTQ):
                    K_.fence()
                    with nc.Block() as blk:
                        emit_ln_tq(K_, blk, t, x_in, tq, t["at_lnb"], T, tq * TQ)

            K_.fence()
            sbuf(K_, es_1, t, "at_wkv", [128, 16 * TQ], BF16)
            sbuf(K_, es_1, t, "at_f32", [128, T], F32)
            sbuf(K_, es_1, t, "at_rt1", [128, T // 2], F32)
            sbuf(K_, es_1, t, "at_rt2", [128, T // 2], F32)
            sbuf(K_, es_1, t, "at_vbf", [128, T], BF16)
            sbuf(K_, es_1, t, "at_wq0", [128, 16 * 128], BF16)
            sbuf(K_, es_1, t, "at_wq1", [128, 16 * 128], BF16)

            ldkv = [0]
            with nc.Block() as blk:
                @blk.sync
                def _(sy):
                    for kc in range(NCH):
                        ldkv[0] = K_.inc(sy.dma_start_transpose(
                            t["at_wkv"][:, kc * TQ:(kc + 1) * TQ],
                            t["wbf_k"][:, kc * 128:(kc + 1) * 128]), "sd", 16)

            for h in range(4):
                with nc.Block() as blk:
                    @blk.tensor
                    def _(tn):
                        tn.wait_ge(K_.sem["sd"], ldkv[0])
                        for kc in range(NCH):
                            for tt in range(NTQ):
                                ins = tn.matmul(
                                    t["ps%d" % tt][:],
                                    t["at_wkv"][:, kc * TQ + h * 128:
                                                kc * TQ + (h + 1) * 128],
                                    t["at_lnb"][:, kc * T + tt * TQ:
                                                kc * T + (tt + 1) * TQ],
                                    start=(kc == 0), stop=(kc == NCH - 1))
                        K_.inc(ins, "pe", 1)

                    @blk.vector
                    def _(v):
                        v.wait_ge(K_.sem["pe"], K_.c["pe"])
                        for tt in range(NTQ):
                            v.tensor_scalar(t["at_f32"][:, tt * TQ:(tt + 1) * TQ],
                                            t["ps%d" % tt][:],
                                            t["sb_bkv"][:, h:h + 1], None,
                                            ALU.add)
                        K_.inc(_emit_rope(K_, v, t, t["at_f32"],
                                          t["at_kbf"][:, h * T:(h + 1) * T]),
                               "dve", 1)

            K_.fence()
            with nc.Block() as blk:
                @blk.sync
                def _(sy):
                    for kc in range(NCH):
                        ldkv[0] = K_.inc(sy.dma_start_transpose(
                            t["at_wkv"][:, kc * TQ:(kc + 1) * TQ],
                            t["wbf_v"][:, kc * 128:(kc + 1) * 128]), "sd", 16)

            vtev = {}
            for h in range(4):
                with nc.Block() as blk:
                    @blk.tensor
                    def _(tn):
                        tn.wait_ge(K_.sem["sd"], ldkv[0])
                        for kc in range(NCH):
                            for tt in range(NTQ):
                                ins = tn.matmul(
                                    t["ps%d" % tt][:],
                                    t["at_wkv"][:, kc * TQ + h * 128:
                                                kc * TQ + (h + 1) * 128],
                                    t["at_lnb"][:, kc * T + tt * TQ:
                                                kc * T + (tt + 1) * TQ],
                                    start=(kc == 0), stop=(kc == NCH - 1))
                        K_.inc(ins, "pe", 1)

                    @blk.vector
                    def _(v):
                        v.wait_ge(K_.sem["pe"], K_.c["pe"])
                        for tt in range(NTQ):
                            ins = v.tensor_scalar(
                                t["at_vbf"][:, tt * TQ:(tt + 1) * TQ],
                                t["ps%d" % tt][:],
                                t["sb_bvv"][:, h:h + 1], None, ALU.add)
                        K_.inc(ins, "dve", 1)

                    for w0 in range(0, NCH, 4):
                        tev = [0]

                        @blk.tensor
                        def _(tn, w0=w0, h=h):
                            psb = t["psbfA"] if (w0 // 4) % 2 == 0 else t["psbfB"]
                            if w0 == 0:
                                tn.wait_ge(K_.sem["dve"], K_.c["dve"])
                            if w0 >= 8:
                                tn.wait_ge(K_.sem["act"], vtev[(h, w0 - 5)])
                            elif h > 0:
                                tn.wait_ge(K_.sem["act"], vtev[(h - 1, w0 + 11)])
                            for kb in range(w0, w0 + 4):
                                ins = tn.transpose(
                                    psb[:, (kb % 4) * 128:(kb % 4 + 1) * 128],
                                    t["at_vbf"][:, kb * 128:(kb + 1) * 128],
                                    t["sb_identbf"][:])
                            tev[0] = K_.inc(ins, "pe", 1)

                        @blk.scalar
                        def _(s, w0=w0, h=h):
                            psb = t["psbfA"] if (w0 // 4) % 2 == 0 else t["psbfB"]
                            s.wait_ge(K_.sem["pe"], tev[0])
                            for kb in range(w0, w0 + 4):
                                vtev[(h, kb)] = K_.inc(s.copy(
                                    t["at_vtok"][:, h * T + kb * 128:
                                                 h * T + (kb + 1) * 128],
                                    psb[:, (kb % 4) * 128:(kb % 4 + 1) * 128]),
                                    "act", 1)

            K_.fence()
            ldq = {}
            with nc.Block() as blk:
                @blk.sync
                def _(sy):
                    for kc in range(NCH):
                        ldq[0] = K_.inc(sy.dma_start_transpose(
                            t["at_wq0"][:, kc * 128:(kc + 1) * 128],
                            t["wbf_q"][0:128, kc * 128:(kc + 1) * 128]),
                            "sd", 16)

            for qh in range(16):
                with nc.Block() as blk:
                    wq = t["at_wq0"] if qh % 2 == 0 else t["at_wq1"]
                    wqn = t["at_wq1"] if qh % 2 == 0 else t["at_wq0"]

                    @blk.sync
                    def _(sy):
                        if qh + 1 < 16:
                            for kc in range(NCH):
                                ldq[qh + 1] = K_.inc(sy.dma_start_transpose(
                                    wqn[:, kc * 128:(kc + 1) * 128],
                                    t["wbf_q"][(qh + 1) * 128:(qh + 2) * 128,
                                               kc * 128:(kc + 1) * 128]),
                                    "sd", 16)

                    @blk.tensor
                    def _(tn):
                        tn.wait_ge(K_.sem["sd"], ldq[qh])
                        for kc in range(NCH):
                            for tt in range(NTQ):
                                ins = tn.matmul(
                                    t["ps%d" % tt][:],
                                    wq[:, kc * 128:(kc + 1) * 128],
                                    t["at_lnb"][:, kc * T + tt * TQ:
                                                kc * T + (tt + 1) * TQ],
                                    start=(kc == 0), stop=(kc == NCH - 1))
                        K_.inc(ins, "pe", 1)

                    @blk.vector
                    def _(v):
                        v.wait_ge(K_.sem["pe"], K_.c["pe"])
                        for tt in range(NTQ):
                            v.tensor_scalar(t["at_f32"][:, tt * TQ:(tt + 1) * TQ],
                                            t["ps%d" % tt][:], RS128,
                                            t["sb_bq_s"][:, qh:qh + 1],
                                            ALU.mult, ALU.add)
                        K_.inc(_emit_rope(K_, v, t, t["at_f32"], t["at_qbf"]),
                               "dve", 1)

                    @blk.gpsimd
                    def _(g):
                        g.wait_ge(K_.sem["dve"], K_.c["dve"])
                        K_.inc(g.dma_start(t["xq_sc"][qh], t["at_qbf"][:]),
                               "gd", 16)

        if t.get("att_dbg"):
            with nc.Block() as blk:
                @blk.gpsimd
                def _(g):
                    for h in range(4):
                        K_.inc(g.dma_start(t["kv_sc"][h],
                                           t["at_kbf"][:, h * T:(h + 1) * T]),
                               "gd", 16)
                        K_.inc(g.dma_start(t["kv_sc"][4 + h],
                                           t["at_vtok"][:, h * T:(h + 1) * T]),
                               "gd", 16)

        K_.fence()
        with ExitStack() as es_2:
            sbuf(K_, es_2, t, "at_pf", [128, T], F32)
            sbuf(K_, es_2, t, "at_pbf", [128, T], BF16)
            sbuf(K_, es_2, t, "at_ptb", [128, 8 * 128], BF16)
            sbuf(K_, es_2, t, "at_ost", [128, T], BF16)
            ost_st = [0]

            qld = {}
            oev = {}
            eevs = {}
            twlast = {}
            owlast = {}
            ptlast = {}
            sevlast = {}
            with nc.Block() as blk:
                for qh in range(16):
                    h = qh // 4
                    qb_cur = t["at_qbf"] if qh % 2 == 0 else t["at_qbf1"]
                    qb_nxt = t["at_qbf1"] if qh % 2 == 0 else t["at_qbf"]

                    @blk.gpsimd
                    def _(g, qh=qh, qb_nxt=qb_nxt):
                        if qh == 0:
                            qld[0] = K_.inc(g.dma_start(t["at_qbf"][:],
                                                        t["xq_sc"][0]), "gd", 16)
                        if qh + 1 < 16:
                            if qh >= 1:
                                g.wait_ge(K_.sem["pe"], sevlast[qh - 1])
                            qld[qh + 1] = K_.inc(g.dma_start(
                                qb_nxt[:], t["xq_sc"][qh + 1]), "gd", 16)

                    for qt in range(NCH):
                        gq = qh * NCH + qt
                        kmax = (qt + 1) * 128
                        nsp = (kmax + TQ - 1) // TQ
                        sb_ = 32 * (gq % 2)
                        sev = [0]
                        mev = [0]
                        eev = [0]
                        pev = [0]
                        smev = [0]
                        rcev = [0]

                        @blk.tensor
                        def _(tn, qh=qh, qt=qt, gq=gq, kmax=kmax, nsp=nsp,
                              h=h, qb_cur=qb_cur):
                            if qt == 0:
                                tn.wait_ge(K_.sem["gd"], qld[qh])
                            if gq >= 1:
                                tn.wait_ge(K_.sem["act"], eevs[gq - 1])
                            for ns in range(nsp):
                                n0 = ns * TQ
                                n1 = min(kmax, n0 + TQ)
                                ins = tn.matmul(
                                    t["ps%d" % ns][:, 0:n1 - n0],
                                    qb_cur[:, qt * 128:(qt + 1) * 128],
                                    t["at_kbf"][:, h * T + n0:h * T + n1],
                                    start=True, stop=True)
                            sev[0] = sevlast[qh] = K_.inc(ins, "pe", 1)

                        @blk.vector
                        def _(v):
                            v.wait_ge(K_.sem["pe"], sev[0])
                            ns_l = (kmax - 1) // TQ
                            off = qt * 128 - ns_l * TQ
                            v.tensor_add(t["ps%d" % ns_l][:, off:off + 128],
                                         t["ps%d" % ns_l][:, off:off + 128],
                                         t["sb_maskc"][:])
                            for ns in range(nsp):
                                n0 = ns * TQ
                                n1 = min(kmax, n0 + TQ)
                                v.reduce_max(t["ps5"][:, sb_ + ns:sb_ + ns + 1],
                                             t["ps%d" % ns][:, 0:n1 - n0], AX.X)
                            v.reduce_max(t["ps5"][:, sb_ + 8:sb_ + 9],
                                         t["ps5"][:, sb_:sb_ + nsp], AX.X)
                            v.tensor_scalar_mul(t["ps5"][:, sb_ + 10:sb_ + 11],
                                                t["ps5"][:, sb_ + 8:sb_ + 9], -1.0)
                            mev[0] = K_.inc(v.tensor_copy(
                                t["at_mx"][:, sb_:sb_ + 32],
                                t["ps5"][:, sb_:sb_ + 32]), "dve", 1)

                        @blk.scalar
                        def _(s):
                            s.wait_ge(K_.sem["dve"], mev[0])
                            for ns in range(nsp):
                                n0 = ns * TQ
                                n1 = min(kmax, n0 + TQ)
                                ins = s.activation(
                                    t["at_pf"][:, n0:n1],
                                    t["ps%d" % ns][:, 0:n1 - n0], AF.Exp,
                                    bias=t["at_mx"][:, sb_ + 10:sb_ + 11])
                            eev[0] = eevs[gq] = K_.inc(ins, "act", 1)

                        @blk.vector
                        def _(v):
                            v.wait_ge(K_.sem["act"], eev[0])
                            ins = v.reduce_sum(t["ps5"][:, sb_ + 12:sb_ + 13],
                                               t["at_pf"][:, 0:kmax], AX.X)
                            smev[0] = K_.inc(ins, "dve", 1)

                        @blk.scalar
                        def _(s):
                            s.wait_ge(K_.sem["dve"], smev[0])
                            s.activation(t["ps5"][:, sb_ + 14:sb_ + 15],
                                         t["ps5"][:, sb_ + 12:sb_ + 13], AF.Ln)
                            s.activation(t["ps5"][:, sb_ + 16:sb_ + 17],
                                         t["ps5"][:, sb_ + 14:sb_ + 15],
                                         AF.Exp, scale=-1.0)
                            rcev[0] = K_.inc(s.copy(t["at_sm"][:, sb_:sb_ + 32],
                                                    t["ps5"][:, sb_:sb_ + 32]),
                                             "act", 1)

                        @blk.vector
                        def _(v):
                            v.wait_ge(K_.sem["act"], rcev[0])
                            if gq >= 1 and (gq - 1) in twlast:
                                v.wait_ge(K_.sem["pe"], twlast[gq - 1])
                            pev[0] = K_.inc(v.tensor_scalar(
                                t["at_pbf"][:, 0:kmax], t["at_pf"][:, 0:kmax],
                                t["at_sm"][:, sb_ + 16:sb_ + 17], None,
                                ALU.mult), "dve", 1)

                        if t.get("att_dbg") and qh == 0:
                            @blk.gpsimd
                            def _(g, qt=qt, kmax=kmax):
                                g.wait_ge(K_.sem["dve"], pev[0])
                                K_.inc(g.dma_start(
                                    t["p_sc"][qt, :, 0:kmax],
                                    t["at_pbf"][:, 0:kmax]), "gd", 16)

                        nkb = qt + 1
                        opev = [0]
                        owev = {}
                        ptev = {}
                        for w0 in range(0, nkb, 4):
                            w1 = min(nkb, w0 + 4)
                            twev = [0]

                            @blk.tensor
                            def _(tn, w0=w0, w1=w1, qt=qt, gq=gq, ptev=ptev):
                                psb = (t["psbfA"] if (w0 // 4) % 2 == 0
                                       else t["psbfB"])
                                tn.wait_ge(K_.sem["dve"], pev[0])
                                if w0 >= 8:
                                    tn.wait_ge(K_.sem["act"], ptev[w0 - 5])
                                elif gq >= 1 and (gq - 1) in ptlast:
                                    tn.wait_ge(K_.sem["act"], ptlast[gq - 1])
                                for kb in range(w0, w1):
                                    ins = tn.transpose(
                                        psb[:, (kb % 4) * 128:(kb % 4 + 1) * 128],
                                        t["at_pbf"][:, kb * 128:(kb + 1) * 128],
                                        t["sb_identbf"][:])
                                twev[0] = twlast[gq] = K_.inc(ins, "pe", 1)

                            @blk.scalar
                            def _(s, w0=w0, w1=w1, qt=qt, gq=gq, ptev=ptev):
                                psb = (t["psbfA"] if (w0 // 4) % 2 == 0
                                       else t["psbfB"])
                                s.wait_ge(K_.sem["pe"], twev[0])
                                if w0 >= 8:
                                    s.wait_ge(K_.sem["pe"], owev[w0 - 8])
                                elif gq >= 1 and (gq - 1) in owlast:
                                    s.wait_ge(K_.sem["pe"], owlast[gq - 1])
                                for kb in range(w0, w1):
                                    ptev[kb] = K_.inc(s.copy(
                                        t["at_ptb"][:, (kb % 8) * 128:
                                                    (kb % 8 + 1) * 128],
                                        psb[:, (kb % 4) * 128:
                                            (kb % 4 + 1) * 128]), "act", 1)
                                ptlast[gq] = ptev[w1 - 1]

                            @blk.tensor
                            def _(tn, w0=w0, w1=w1, qt=qt, gq=gq, ptev=ptev):
                                if w0 == 0 and gq >= 1:
                                    tn.wait_ge(K_.sem["act"], oev[gq - 1])
                                for kb in range(w0, w1):
                                    tn.wait_ge(K_.sem["act"], ptev[kb])
                                    ins = tn.matmul(
                                        t["ps4"][:, 0:128],
                                        t["at_vtok"][:, h * T + kb * 128:
                                                     h * T + (kb + 1) * 128],
                                        t["at_ptb"][:, (kb % 8) * 128:
                                                    (kb % 8 + 1) * 128],
                                        start=(kb == 0), stop=(kb == nkb - 1))
                                owev[w0] = owlast[gq] = opev[0] = K_.inc(ins, "pe", 1)

                        @blk.scalar
                        def _(s, qt=qt, gq=gq):
                            s.wait_ge(K_.sem["pe"], opev[0])
                            if qt == 0:
                                s.wait_ge(K_.sem["gd"], ost_st[0])
                            oev[gq] = K_.inc(s.copy(
                                t["at_ost"][:, qt * 128:(qt + 1) * 128],
                                t["ps4"][:, 0:128]), "act", 1)

                    @blk.gpsimd
                    def _(g, qh=qh):
                        g.wait_ge(K_.sem["act"], oev[qh * NCH + NCH - 1])
                        ost_st[0] = K_.inc(g.dma_start(t["xo_sc"][qh],
                                                       t["at_ost"][:]), "gd", 16)

        K_.fence()
        with ExitStack() as es_3:
            sbuf(K_, es_3, t, "at_wo", [128, 16 * T], BF16)
            sbuf(K_, es_3, t, "at_otq", [128, 16 * TQ], BF16)
            sbuf(K_, es_3, t, "at_xo", [128, 4 * TQ], F32)
            sbuf(K_, es_3, t, "at_xn", [128, 4 * TQ], F32)
            ldo = [0]
            with nc.Block() as blk:
                @blk.sync
                def _(sy):
                    for ms in range(4):
                        for kc in range(NCH):
                            ldo[0] = K_.inc(sy.dma_start_transpose(
                                t["at_wo"][:, kc * T + ms * 512:
                                           kc * T + (ms + 1) * 512],
                                t["wbf_o"][ms * 512:(ms + 1) * 512,
                                           kc * 128:(kc + 1) * 128]), "sd", 16)

            for tq in range(NTQ):
                with nc.Block() as blk:
                    xld = {}
                    old = {}
                    stv = {}
                    zpe = {}
                    b_dve_wo = K_.c["dve"]

                    sd0_wo = K_.c["sd"]

                    @blk.gpsimd
                    def _(g):
                        for kc in range(NCH):
                            old[kc] = K_.inc(g.dma_start(
                                t["at_otq"][:, kc * TQ:(kc + 1) * TQ],
                                t["xo_sc"][kc, :, tq * TQ:(tq + 1) * TQ]),
                                "gd", 16)
                        for mc in range(NCH):
                            if mc >= 4:
                                g.wait_ge(K_.sem["dve"], b_dve_wo + mc - 3)
                            xld[mc] = K_.inc(g.dma_start(
                                t["at_xo"][:, (mc % 4) * TQ:(mc % 4 + 1) * TQ],
                                x_in[mc * 128:(mc + 1) * 128,
                                     tq * TQ:(tq + 1) * TQ]), "gd", 16)

                    @blk.tensor
                    def _(tn):
                        tn.wait_ge(K_.sem["sd"], ldo[0])
                        tn.wait_ge(K_.sem["gd"], old[NCH - 1])
                        for mc in range(NCH):
                            psZ = t["ps0"] if mc % 2 == 0 else t["ps1"]
                            if mc >= 2:
                                tn.wait_ge(K_.sem["dve"], b_dve_wo + mc - 1)
                            for kc in range(NCH):
                                ins = tn.matmul(
                                    psZ[:],
                                    t["at_wo"][:, kc * T + mc * 128:
                                               kc * T + (mc + 1) * 128],
                                    t["at_otq"][:, kc * TQ:(kc + 1) * TQ],
                                    start=(kc == 0), stop=(kc == NCH - 1))
                            zpe[mc] = K_.inc(ins, "pe", 1)

                    stq = {}

                    @blk.vector
                    def _(v):
                        for mc in range(NCH):
                            psZ = t["ps0"] if mc % 2 == 0 else t["ps1"]
                            v.wait_ge(K_.sem["pe"], zpe[mc])
                            v.wait_ge(K_.sem["gd"], xld[mc])
                            if mc >= 4:
                                v.wait_ge(K_.sem["sd"], sd0_wo + 16 * (mc - 3))
                            stv[mc] = K_.inc(v.scalar_tensor_tensor(
                                t["at_xn"][:, (mc % 4) * TQ:(mc % 4 + 1) * TQ],
                                psZ[:], t["sb_bov"][:, mc:mc + 1],
                                t["at_xo"][:, (mc % 4) * TQ:(mc % 4 + 1) * TQ],
                                ALU.add, ALU.add), "dve", 1)

                    @blk.sync
                    def _(sy):
                        for mc in range(NCH):
                            sy.wait_ge(K_.sem["dve"], stv[mc])
                            stq[mc] = K_.inc(sy.dma_start(
                                x_out[mc * 128:(mc + 1) * 128,
                                      tq * TQ:(tq + 1) * TQ],
                                t["at_xn"][:, (mc % 4) * TQ:(mc % 4 + 1) * TQ]),
                                "sd", 16)




WSHAPES = {
    "wup": [H0, D], "wgate": [H0, D], "wdown": [D, H0],
    "wq": [D, D], "wk": [KV, D], "wv": [KV, D], "wo": [D, D],
    "wl1": [LQG, D], "w1": [H1, D], "w2": [H1, D], "w3": [D, H1],
}
BFSHAPES = {
    "wbf_up": [H0, D], "wbf_gate": [H0, D], "wbf_down": [D, H0],
    "wbf_q": [D, D], "wbf_k": [KV, D], "wbf_v": [KV, D], "wbf_o": [D, D],
    "wbf_l1": [LQG, D], "wbf_w1": [H1, D], "wbf_w2": [H1, D], "wbf_w3": [D, H1],
}
NAMEMAP = {"wup": "wbf_up", "wgate": "wbf_gate", "wdown": "wbf_down",
           "wq": "wbf_q", "wk": "wbf_k", "wv": "wbf_v", "wo": "wbf_o",
           "wl1": "wbf_l1", "w1": "wbf_w1", "w2": "wbf_w2", "w3": "wbf_w3"}
VECNAMES = ("bqv", "bkv", "bvv", "bov", "suug", "sudp", "wscu", "wscg", "wscdp")
CONSTNAMES = ("ident", "h128", "cosx", "sinx", "maskc", "ones_a", "ones_b")


def build_program(sub_ids):
    nc = bass.Bass("TRN2", target_bir_lowering=False)
    need = set(SUBLAYERS[i] for i in sub_ids)
    t = {}
    innames = ["x_in"]
    t["x_in"] = nc.dram_tensor("x_in", [D, T], F32, kind="ExternalInput").ap()
    t["y_out"] = nc.dram_tensor("y_out", [D, T], F32, kind="ExternalOutput").ap()
    wneed = []
    if "ffn0" in need:
        wneed += ["wup", "wgate", "wdown"]
    if "att" in need:
        wneed += ["wq", "wk", "wv", "wo"]
    if "scan" in need:
        wneed += ["wl1"]
    if "ffn1" in need:
        wneed += ["w1", "w2", "w3"]
    for w in wneed:
        t[w] = nc.dram_tensor(w, WSHAPES[w], F32, kind="ExternalInput").ap()
        innames.append(w)
    for name in VECNAMES:
        shape = {"bqv": [128, 16], "bkv": [128, 4], "bvv": [128, 4],
                 "bov": [128, 16], "suug": [128, 16], "sudp": [128, 64],
                 "wscu": [128, 64], "wscg": [128, 64], "wscdp": [128, 16]}[name]
        t[name] = nc.dram_tensor(name, shape, F32, kind="ExternalInput").ap()
        innames.append(name)
    for name in CONSTNAMES:
        shape = {"ident": [128, 128], "h128": [128, 128], "cosx": [128, T],
                 "sinx": [128, T], "maskc": [128, 128], "ones_a": [128, 1],
                 "ones_b": [1, 128]}[name]
        t[name] = nc.dram_tensor(name, shape, F32, kind="ExternalInput").ap()
        innames.append(name)
    qdbg = os.environ.get("QDBG", "0") == "1"
    if qdbg:
        t["qdbg"] = nc.dram_tensor("qdbg", [128, 1024], F32,
                                   kind="ExternalOutput").ap()
    for w in wneed:
        n = NAMEMAP[w]
        kk = "ExternalOutput" if (qdbg and n in ("wbf_w1", "wbf_w3")) else "Internal"
        t[n] = nc.dram_tensor(n, BFSHAPES[n], BF16, kind=kk).ap()
    if len(sub_ids) > 1:
        t["xA"] = nc.dram_tensor("xA", [D, T], F32, kind="Internal").ap()
        t["xB"] = nc.dram_tensor("xB", [D, T], F32, kind="Internal").ap()
    if "att" in need:
        dbg = os.environ.get("ATT_DBG", "0") == "1"
        kq = "ExternalOutput" if dbg else "Internal"
        t["att_dbg"] = dbg
        t["xq_sc"] = nc.dram_tensor("xq_sc", [16, 128, T], BF16, kind=kq).ap()
        t["xo_sc"] = nc.dram_tensor("xo_sc", [16, 128, T], BF16, kind=kq).ap()
        if dbg:
            t["kv_sc"] = nc.dram_tensor("kv_sc", [8, 128, T], BF16,
                                        kind="ExternalOutput").ap()
            t["p_sc"] = nc.dram_tensor("p_sc", [16, 128, T], BF16,
                                       kind="ExternalOutput").ap()

    with ExitStack() as es:
        K_ = K(nc, es)
        for i_ in range(6):
            t["ps%d" % i_] = es.enter_context(
                nc.psum_tensor("ps%d" % i_, [128, 512], F32))
        t["psbfA"] = es.enter_context(nc.psum_tensor("psbfA", [128, 512], BF16))
        t["psbfB"] = es.enter_context(nc.psum_tensor("psbfB", [128, 512], BF16))
        sbuf(K_, es, t, "sb_ones_a", [128, 1], F32)
        sbuf(K_, es, t, "sb_ones_b", [1, 128], F32)
        sbuf(K_, es, t, "sb_identbf", [128, 128], BF16)
        sbuf(K_, es, t, "sb_h128", [128, 128], F32)
        sbuf(K_, es, t, "sb_h128bf", [128, 128], BF16)
        sbuf(K_, es, t, "sb_h128su", [128, 16 * 128], BF16)
        sbuf(K_, es, t, "sb_maskc", [128, 128], F32)
        for name in VECNAMES:
            shp = {"bqv": 16, "bkv": 4, "bvv": 4, "bov": 16, "suug": 16,
                   "sudp": 64, "wscu": 64, "wscg": 64, "wscdp": 16}[name]
            sbuf(K_, es, t, "sb_" + name, [128, shp], F32)
        sbuf(K_, es, t, "sb_bq_s", [128, 16], F32)
        sbuf(K_, es, t, "sb_wscu_su", [128, 64], F32)
        sbuf(K_, es, t, "sb_wscg_s", [128, 64], F32)
        sbuf(K_, es, t, "sb_wscdp_h", [128, 16], F32)
        sbuf(K_, es, t, "sb_eps", [1, 1], F32)
        if "att" in need:
            sbuf(K_, es, t, "sb_cos", [128, T], BF16)
            sbuf(K_, es, t, "sb_sin", [128, T], BF16)

        p0_emit(K_, t, es, need)

        cur = t["x_in"]
        n = len(sub_ids)
        for i_, si in enumerate(sub_ids):
            last = (i_ == n - 1)
            nxt = t["y_out"] if last else (t["xA"] if i_ % 2 == 0 else t["xB"])
            kind = SUBLAYERS[si]
            if kind == "ffn0":
                emit_ffn0(K_, t, cur, nxt)
            elif kind == "ffn1":
                emit_ffn1(K_, t, cur, nxt)
            elif kind == "scan":
                emit_scan(K_, t, cur, nxt)
            elif kind == "att":
                emit_att(K_, t, cur, nxt)
            cur = nxt

        K_.fence()
    return nc, innames




def make_base_map(inputs):
    base = dict(host_consts())
    base["bqv"] = vec_chunks(inputs["L0_attn_bq"])
    base["bkv"] = vec_chunks(inputs["L0_attn_bk"])
    base["bvv"] = vec_chunks(inputs["L0_attn_bv"])
    base["bov"] = vec_chunks(inputs["L0_attn_bo"])
    base["suug"] = vec_chunks(inputs["L0_ffn_SU_ug"])
    base["sudp"] = vec_chunks(inputs["L0_ffn_SU_dp"])
    wsc = np.asarray(inputs["L0_ffn_Wscale_ug"], dtype=np.float32)
    base["wscu"] = vec_chunks(wsc[:H0])
    base["wscg"] = vec_chunks(wsc[H0:])
    base["wscdp"] = vec_chunks(inputs["L0_ffn_Wscale_dp"])
    for name, key in (("wup", "L0_ffn_W_up"), ("wgate", "L0_ffn_W_gate"),
                      ("wdown", "L0_ffn_W_down"), ("wq", "L0_attn_Wq"),
                      ("wk", "L0_attn_Wk"), ("wv", "L0_attn_Wv"),
                      ("wo", "L0_attn_Wo"), ("wl1", "L1_attn_W"),
                      ("w1", "L1_ffn_w1"), ("w2", "L1_ffn_w2"),
                      ("w3", "L1_ffn_w3")):
        base[name] = np.ascontiguousarray(np.asarray(inputs[key],
                                                     dtype=np.float32))
    return base


def kernel(**inputs):
    base = make_base_map(inputs)
    x = np.asarray(inputs["L0_attn_x"], dtype=np.float32)
    x_cur = [np.ascontiguousarray(x[c].T) for c in range(8)]
    groups = ([[i] for i in range(6)] if UNFUSED else [list(range(6))])
    for g in groups:
        nc, innames = build_program(g)
        in_maps = []
        for c in range(8):
            m = {k: base[k] for k in innames if k != "x_in"}
            m["x_in"] = x_cur[c]
            in_maps.append(m)
        res = run_bass_kernel_spmd(nc, in_maps, core_ids=list(range(8)))
        x_cur = [np.asarray(res.results[c]["y_out"]) for c in range(8)]
    return np.stack([x_cur[c].T for c in range(8)], axis=0).astype(np.float32)
